# version 13
# baseline (speedup 1.0000x reference)
import math
import numpy as np
import ml_dtypes
import concourse.bass as bass
import concourse.mybir as mybir
from concourse.bass_utils import run_bass_kernel_spmd

F32 = mybir.dt.float32
BF16 = mybir.dt.bfloat16
AF = mybir.ActivationFunctionType
ALU = mybir.AluOpType
AX = mybir.AxisListType

NCORES = 8
BIG = 1.0e9
INTERLEAVE = 1
ENGINES = ("tensor", "vector", "scalar", "gpsimd", "sync")


class _Op:
    __slots__ = ("eng", "emit", "ins", "outs", "inc", "dma_sem", "waits", "event", "label")

    def __init__(self, eng, emit, ins, outs, inc=True, dma_sem=None, label=""):
        self.eng = eng
        self.emit = emit
        self.ins = list(ins)
        self.outs = list(outs)
        self.inc = inc
        self.dma_sem = dma_sem
        self.waits = []
        self.event = None
        self.label = label


class Prog:
    def __init__(self):
        self.ops = []
        self.lw = {}
        self.rd = {}
        self.acq = {e: {} for e in ENGINES}
        self.rel = {}
        self.cnt = {}
        self.pend = {e: [] for e in ENGINES}
        self.pend_reads = {e: [] for e in ENGINES}
        self.pool_idx = {}
        self.esem = {e: f"E{e[0]}" for e in ENGINES}
        for e in ENGINES:
            self.cnt[self.esem[e]] = 0

    def _need(self, eng, ev):
        return self.acq[eng].get(ev[0], 0) < ev[1]

    def _acquire(self, eng, ev):
        a = self.acq[eng]
        for s, v in self.rel.get(ev, {}).items():
            if a.get(s, 0) < v:
                a[s] = v
        if a.get(ev[0], 0) < ev[1]:
            a[ev[0]] = ev[1]

    def _issue(self, op):
        eng = op.eng
        deps = []
        for k in op.ins:
            ev = self.lw.get(k)
            if ev is not None:
                deps.append((ev, k))
        for k in op.outs:
            ev = self.lw.get(k)
            if ev is not None:
                deps.append((ev, k))
            for rev in self.rd.get(k, ()):
                deps.append((rev, k))
        for ev, key in deps:
            if isinstance(ev, tuple) and ev[0] == "pend":
                if ev[1] != eng:
                    raise RuntimeError(f"cross-engine pending dep on {key} ({ev[1]} -> {eng}) at {op.label}")
                continue
            if ev[0].startswith("D"):
                ev = (ev[0], self.cnt[ev[0]])
            if self._need(eng, ev):
                op.waits.append(ev)
                self._acquire(eng, ev)
        if op.dma_sem is not None:
            sem = op.dma_sem
            self.cnt[sem] = self.cnt.get(sem, 0) + 16
            op.event = (sem, self.cnt[sem])
        elif op.inc:
            sem = self.esem[eng]
            self.cnt[sem] += 1
            op.event = (sem, self.cnt[sem])
        if op.event is not None:
            snap = dict(self.acq[eng])
            snap[op.event[0]] = op.event[1]
            self.rel[op.event] = snap
            for k in self.pend[eng]:
                self.lw[k] = op.event
                self.rd[k] = []
            self.pend[eng] = []
            for k in self.pend_reads[eng]:
                self.rd.setdefault(k, []).append(op.event)
            self.pend_reads[eng] = []
            for k in op.outs:
                self.lw[k] = op.event
                self.rd[k] = []
            for k in op.ins:
                self.rd.setdefault(k, []).append(op.event)
        else:
            for k in op.outs:
                self.lw[k] = ("pend", eng)
                self.rd[k] = []
                self.pend[eng].append(k)
            self.pend_reads[eng].extend(op.ins)
        self.ops.append(op)

    def op(self, eng, emit, ins=(), outs=(), inc=True, label=""):
        self._issue(_Op(eng, emit, ins, outs, inc=inc, label=label))

    def dma(self, eng, emit, ins=(), outs=(), role="d", label=""):
        sem = f"D{role}"
        if sem not in self.cnt:
            self.cnt[sem] = 0
        o = _Op(eng, emit, ins, outs, dma_sem=sem, label=label)
        if self.cnt[sem] > 0:
            prev = (sem, self.cnt[sem])
            if self._need(eng, prev):
                o.waits.append(prev)
                self._acquire(eng, prev)
        self._issue(o)

    def dma_pool(self, eng, emit, ins=(), outs=(), pool="p", npool=4, label=""):
        i = self.pool_idx.get(pool, 0)
        self.pool_idx[pool] = (i + 1) % npool
        self.dma(eng, emit, ins, outs, role=f"{pool}{i}", label=label)

    def hard_barrier(self, label="barrier"):
        tot = dict(self.cnt)
        for eng in ENGINES:
            for sem, val in tot.items():
                if val > 0 and self.acq[eng].get(sem, 0) < val:
                    o = _Op(eng, None, [], [], inc=False, label=label)
                    o.waits = [(sem, val)]
                    self.ops.append(o)
                    self._acquire(eng, (sem, val))

    def final_waits(self, eng="sync"):
        for sem, val in self.cnt.items():
            if sem.startswith("D") and val > 0 and self.acq[eng].get(sem, 0) < val:
                o = _Op(eng, None, [], [], inc=False, label="final")
                o.waits = [(sem, val)]
                self.ops.append(o)
                self._acquire(eng, (sem, val))

    maxop = None

    def replay(self, nc, block, sems):
        def emit_stream(name):
            def run(eobj):
                for oi, op in enumerate(self.ops):
                    if self.maxop is not None and oi >= self.maxop:
                        break
                    if op.eng != name:
                        continue
                    for ev in op.waits:
                        eobj.wait_ge(sems[ev[0]], ev[1])
                    if op.emit is None:
                        continue
                    instr = op.emit(eobj)
                    if op.event is not None:
                        step = 16 if op.dma_sem is not None else 1
                        instr.then_inc(sems[op.event[0]], step)
            return run
        block.tensor(emit_stream("tensor"))
        block.vector(emit_stream("vector"))
        block.scalar(emit_stream("scalar"))
        block.gpsimd(emit_stream("gpsimd"))
        block.sync(emit_stream("sync"))

    def used_sems(self):
        s = set()
        for op in self.ops:
            for ev in op.waits:
                s.add(ev[0])
            if op.event is not None:
                s.add(op.event[0])
        return sorted(s)


def build_nc(T=8192, D=2048, dbg=False, maxop=None):
    DC = D // 128
    SEG = T // 4
    NT = SEG // 128
    NCK = NT
    TCH = min(512, SEG)
    NTCH = SEG // TCH
    NSEG = 4
    FS = D // 8
    invs8 = 0.5 / math.sqrt(8.0)
    DKmh = 128.0 ** (-0.5)

    nc = bass.Bass("TRN2", target_bir_lowering=False, debug=False)

    def din(name, shape, dt=F32):
        return nc.dram_tensor(name, shape, dt, kind="ExternalInput").ap()

    xsh = din("xsh", [SEG, D])
    wqkv_s = din("wqkv_s", [D, 512])
    wz_s = din("wz_s", [D, 256])
    wab_s = din("wab_s", [D, 4])
    fact_s = din("fact_s", [8, D, FS])
    fbias = din("fbias", [1, D])
    wout_in = din("wout_in", [2048, D])
    convw_s = din("convw_s", [512, 4])
    dtb_s = din("dtb_s", [1, 2])
    alog_s = din("alog_s", [1, 2])
    normw = din("normw", [1, 128])
    IDbf = din("IDbf", [128, 128], BF16)
    IDf = din("IDf", [128, 128], F32)
    UINC = din("UINC", [128, 128], F32)
    MASKU = din("MASKU", [128, 128], F32)
    MASKL = din("MASKL", [128, 128], F32)
    ONESROW = din("ONESROW", [1, 128], F32)
    ONECOL = din("ONECOL", [128, 1], F32)

    ysh = nc.dram_tensor("ysh", [SEG, D], F32, kind="ExternalOutput").ap()

    fsag_in = nc.dram_tensor("fsag_in", [D, FS], BF16, kind="Internal").ap()
    fsag_out = nc.dram_tensor("fsag_out", [NCORES * D, FS], BF16, kind="Internal", addr_space="Shared").ap()
    H2P = 1
    HSEG = SEG // H2P
    h2ag_ins = [nc.dram_tensor(f"h2ag_in{p}", [HSEG, D], BF16, kind="Internal").ap()
                for p in range(H2P)]
    h2ag_outs = [nc.dram_tensor(f"h2ag_out{p}", [NCORES * HSEG, D], BF16, kind="Internal",
                                addr_space="Shared").ap() for p in range(H2P)]
    ogag_in = nc.dram_tensor("ogag_in", [256, 2 * T], BF16, kind="Internal").ap()
    ogag_out = nc.dram_tensor("ogag_out", [NCORES * 256, 2 * T], BF16, kind="Internal", addr_space="Shared").ap()
    x1sp = nc.dram_tensor("x1sp", [SEG, D], F32, kind="Internal").ap()
    x2sp = nc.dram_tensor("x2sp", [SEG, D], F32, kind="Internal").ap()
    dbgt = nc.dram_tensor("dbgt", [40, 128, 128], F32, kind="Internal").ap() if dbg else None

    from contextlib import ExitStack
    es = ExitStack()

    def sb(name, shape, dt):
        return es.enter_context(nc.sbuf_tensor(name, shape, dt))

    idbf = sb("idbf", [128, 128], BF16)
    idf = sb("idf", [128, 128], F32)
    masku = sb("masku", [128, 128], F32)
    maskl = sb("maskl", [128, 128], F32)
    onesrow = sb("onesrow", [1, 128], F32)
    biasb = sb("biasb", [128, D], F32)
    nwb = sb("nwb", [128, 128], F32)
    dtbb = sb("dtbb", [128, 2], F32)
    nalogb = sb("nalogb", [128, 2], F32)
    srw = sb("srw", [1, 256], F32)
    convw = sb("convw", [128, 16], F32)
    gpk = sb("gpk", [128, 8 * NCK * 4], F32)
    epsc = sb("epsc", [128, 3], F32)
    bgs = sb("bgs", [128, 512], F32)
    gtmp = sb("gtmp", [128, 2 * 64], F32)
    gd2 = [sb(f"gd2_{s}", [128, 2], F32) for s in range(2)]
    grow2 = [sb(f"grow2_{s}", [2, 128], F32) for s in range(2)]
    gcs2 = [sb(f"gcs2_{s}", [2, 128], F32) for s in range(2)]
    ones2 = sb("ones2", [2, 128], F32)
    ones128 = sb("ones128", [128, 128], F32)
    lnsc = sb("lnsc", [128, 8], F32)
    lnsc2 = sb("lnsc2", [128, 8], F32)

    x0t = sb("x0t", [128, D], F32)
    x0t2 = sb("x0t2", [128, D], F32)
    x1t = sb("x1t", [128, D], F32)
    hbf = sb("hbf", [128, D], BF16)

    S_f = [sb(f"Sf{i}", [128, 128], F32) for i in range(4)]
    S_b = [sb(f"Sb{i}", [128, 128], BF16) for i in range(4)]
    E1 = [sb(f"E1_{i}", [128, 128], F32) for i in range(4)]
    E2 = [sb(f"E2_{i}", [128, 128], F32) for i in range(4)]
    Xb = [sb(f"Xb{i}", [128, 128], BF16) for i in range(4)]
    Yb = [sb(f"Yb{i}", [128, 128], BF16) for i in range(4)]
    PYa = [sb(f"PYa{i}", [128, 256], BF16) for i in range(4)]
    PYb = [sb(f"PYb{i}", [128, 256], BF16) for i in range(4)]
    Xs = [sb(f"Xs{i}", [128, 128], BF16) for i in range(4)]
    Vbs = [[sb(f"Vbs{i}_{p}", [128, 128], BF16) for p in range(2)] for i in range(4)]
    Ktok = [[sb(f"Ktok{i}_{p}", [128, 128], BF16) for p in range(2)] for i in range(4)]
    Rb = [sb(f"Rb{i}", [128, 128], BF16) for i in range(4)]
    Ub = [sb(f"Ub{i}", [128, 128], BF16) for i in range(4)]
    Upb = [sb(f"Upb{i}", [128, 128], BF16) for i in range(4)]
    Gb = [[sb(f"Gb{i}_{p}", [128, 128], BF16) for p in range(2)] for i in range(4)]
    Pf = [[sb(f"Pf{i}_{p}", [128, 128], BF16) for p in range(2)] for i in range(4)]
    Otmp = [sb(f"Otmp{i}", [128, 128], F32) for i in range(4)]
    Of = [sb(f"Of{i}", [128, 128], F32) for i in range(4)]
    cols = [sb(f"cols{i}", [128, 32], F32) for i in range(4)]

    wqkvb = sb("wqkvb", [128, DC * 512], BF16)
    wzb = sb("wzb", [128, DC * 256], BF16)
    wabb = sb("wabb", [128, DC * 4], BF16)

    a1w = max((DC * FS) + (DC * D // 2) + 5 * D // 2,
              2 * (DC * TCH // 2) + (8 * SEG // 2) + (4 * SEG // 2) + 4 * ((TCH + 8) // 2) + (4 * SEG // 2) + (16 * 128 // 2),
              (16 * D // 2) + 2 * (16 * 128 // 2),
              (DC * D // 2))
    AR1 = sb("AR1", [128, a1w], F32)
    AR1b = AR1.bitcast(BF16)

    def a1f(off, colsf):
        return AR1[:, off:off + colsf]

    def a1b(off, colsb):
        return AR1b[:, 2 * off:2 * off + colsb]

    facc = lambda: a1f(0, DC * FS)
    fsumA = lambda: a1b(DC * FS, DC * D)
    o_h2 = 0
    o_qkvT = o_h2 + 2 * (DC * TCH // 2)
    o_zsT = o_qkvT + (8 * SEG) // 2
    o_stag = o_zsT + (4 * SEG) // 2
    o_ogsb = o_stag + 4 * ((TCH + 8) // 2)
    o_convd = o_ogsb + (4 * SEG) // 2
    h2tb = lambda i: a1b(o_h2 + i * (DC * TCH // 2), DC * TCH)
    qkvT = lambda: a1b(o_qkvT, 8 * SEG)
    zsT = lambda: a1b(o_zsT, 4 * SEG)
    stagW = lambda cc: a1b(o_stag + cc * ((TCH + 8) // 2), TCH + 8)
    ogsb = lambda: a1b(o_ogsb, 4 * SEG)
    convd = lambda: a1b(o_convd, 16 * 128)
    woutb = lambda: a1b(0, 16 * D)
    ogt2_o = 16 * D // 2
    fsumC = lambda: a1b(0, DC * D)
    o_par = DC * FS + (DC * D) // 2
    htT_p = [a1b(o_par, DC * 128), a1b(o_par + D // 2, DC * 128)]
    hbf_p = [None, a1b(o_par + D, D)]
    x1t_p = [None, None]



    dbgsb = sb("dbgsb", [128, 40 * 128], F32) if dbg else None
    bk = [es.enter_context(nc.psum_tensor(f"bk{i}", [128, 512], F32)) for i in range(8)]
    bkb = [b.bitcast(BF16) for b in bk]
    hbf_p[0] = hbf[:]
    x1t_p[0] = x1t[:]
    x1t_p[1] = a1f(o_par + 3 * D // 2, D)
    lnsc_p = [lnsc, lnsc2]

    P = Prog()
    rg = [[i for i in range(NCORES)]]

    P.dma("sync", lambda e: e.dma_start(out=idbf[:], in_=IDbf), outs=["idbf"], role="c0")
    P.dma("sync", lambda e: e.dma_start(out=idf[:], in_=IDf), outs=["idf"], role="c0")
    P.dma("sync", lambda e: e.dma_start(out=masku[:], in_=MASKU), outs=["masku"], role="c1")
    P.dma("sync", lambda e: e.dma_start(out=maskl[:], in_=MASKL), outs=["maskl"], role="c1")
    P.dma("sync", lambda e: e.dma_start(out=onesrow[:], in_=ONESROW), outs=["onesrow"], role="c1")
    brow = x1t[0:1, 0:D]
    P.dma("sync", lambda e: e.dma_start(out=brow, in_=fbias), outs=["x1t"], role="c2")
    P.dma("sync", lambda e: e.dma_start(out=srw[0:1, 0:2], in_=dtb_s), outs=["srw"], role="c2")
    P.dma("sync", lambda e: e.dma_start(out=srw[0:1, 4:6], in_=alog_s), outs=["srw"], role="c2")
    P.dma("sync", lambda e: e.dma_start(out=srw[0:1, 8:136], in_=normw), outs=["srw"], role="c2")
    for q in range(4):
        P.dma("gpsimd", lambda e, q=q: e.dma_start(out=convw[:, q * 4:(q + 1) * 4],
                                                   in_=convw_s[q * 128:(q + 1) * 128, :]),
              outs=["convw"], role="g0")

    if dbg:
        P.op("vector", lambda e: e.memset(dbgsb[:], 0.0), ins=[], outs=["dbgsb"], label="dbg0")
    P.op("vector", lambda e: e.memset(epsc[:, 0:1], 1e-5), ins=[], outs=["epsc"], label="eps5")
    P.op("vector", lambda e: e.memset(epsc[:, 1:2], 1e-6), ins=[], outs=["epsc"], label="eps6")
    P.op("vector", lambda e: e.memset(epsc[:, 2:3], 1.0), ins=[], outs=["epsc"], label="one")
    P.op("vector", lambda e: e.memset(ones2[:], 1.0), ins=[], outs=["ones2"], label="ones2")
    P.op("vector", lambda e: e.memset(ones128[:], 1.0), ins=[], outs=["ones128"], label="ones128")
    P.op("scalar", lambda e: e.activation(srw[0:1, 6:8], srw[0:1, 4:6], AF.Exp),
         ins=["srw"], outs=["srw"], label="expalog")
    P.op("vector", lambda e: e.tensor_scalar(srw[0:1, 6:8], srw[0:1, 6:8], -1.0, None, ALU.mult),
         ins=["srw"], outs=["srw"], label="negalog")

    for j in range((D + 511) // 512):
        w = min(512, D - j * 512)
        P.op("tensor", lambda e, j=j, w=w: e.matmul(bk[0][:, 0:w], onesrow[:],
                                                    x1t[0:1, j * 512:j * 512 + w], start=True, stop=True),
             ins=["onesrow", "x1t"], outs=["bk0"], label="biasbc")
        P.op("vector", lambda e, j=j, w=w: e.tensor_scalar(biasb[:, j * 512:j * 512 + w],
                                                           bk[0][:, 0:w], 0.5, None, ALU.mult),
             ins=["bk0"], outs=["biasb"], label="biasev")
    P.op("tensor", lambda e: e.matmul(bk[0][:, 0:128], onesrow[:], srw[0:1, 8:136], start=True, stop=True),
         ins=["onesrow", "srw"], outs=["bk0"], label="nwbc")
    P.op("vector", lambda e: e.tensor_copy(nwb[:], bk[0][:, 0:128]), ins=["bk0"], outs=["nwb"])
    P.op("tensor", lambda e: e.matmul(bk[0][:, 0:2], onesrow[:], srw[0:1, 0:2], start=True, stop=True),
         ins=["onesrow", "srw"], outs=["bk0"], label="dtbbc")
    P.op("vector", lambda e: e.tensor_copy(dtbb[:], bk[0][:, 0:2]), ins=["bk0"], outs=["dtbb"])
    P.op("tensor", lambda e: e.matmul(bk[0][:, 0:2], onesrow[:], srw[0:1, 6:8], start=True, stop=True),
         ins=["onesrow", "srw"], outs=["bk0"], label="nalogbc")
    P.op("vector", lambda e: e.tensor_copy(nalogb[:], bk[0][:, 0:2]), ins=["bk0"], outs=["nalogb"])

    P.dma("gpsimd", lambda e: e.dma_start(
        out=wqkvb[:].rearrange("p (dc c) -> p dc c", c=512),
        in_=wqkv_s.rearrange("(dc p) c -> p dc c", p=128)), outs=["wqkvb"], role="g1")
    P.dma("gpsimd", lambda e: e.dma_start(
        out=wzb[:].rearrange("p (dc c) -> p dc c", c=256),
        in_=wz_s.rearrange("(dc p) c -> p dc c", p=128)), outs=["wzb"], role="g2")
    P.dma("gpsimd", lambda e: e.dma_start(
        out=wabb[:].rearrange("p (dc c) -> p dc c", c=4),
        in_=wab_s.rearrange("(dc p) c -> p dc c", p=128)), outs=["wabb"], role="g2")

    qdc = max(1, DC // 4)
    NQ = DC // qdc
    qw = qdc * FS
    for r in range(8):
        kw = {} if r == 0 else dict(accum_op=ALU.add)
        for q in range(NQ):
            P.dma("gpsimd", lambda e, r=r, q=q, kw=kw: e.dma_start(
                out=facc().rearrange("p (dc c) -> p dc c", c=FS)[:, q * qdc:(q + 1) * qdc, :],
                in_=fact_s[r][q * qdc * 128:(q + 1) * qdc * 128, :].rearrange("(dc p) c -> p dc c", p=128),
                **kw),
                ins=([f"facc{q}"] if r else []), outs=[f"facc{q}"], role=f"fa{q}", label=f"fact{q}_{r}")
    nsc = (DC * FS + 511) // 512
    for j in range(nsc):
        w = min(512, DC * FS - j * 512)
        ks = [f"facc{q}" for q in range((j * 512) // qw, min(NQ - 1, (j * 512 + w - 1) // qw) + 1)]
        P.op("vector", lambda e, j=j, w=w: e.tensor_scalar(
            facc()[:, j * 512:j * 512 + w], facc()[:, j * 512:j * 512 + w], invs8, None, ALU.mult),
            ins=ks, outs=ks, label="fscale")
    P.dma("gpsimd", lambda e: e.dma_start(
        out=fsag_in.rearrange("(dc p) c -> p dc c", p=128),
        in_=facc().rearrange("p (dc c) -> p dc c", c=FS)),
        ins=[f"facc{q}" for q in range(NQ)], outs=["fsag_in"], role="g3")
    P.op("gpsimd", lambda e: e.collective_compute("AllGather", ALU.bypass, rg, [fsag_in], [fsag_out]),
         ins=["fsag_in"], outs=["fsag_out"], label="AGfsum")
    for r in range(NCORES):
        P.dma("gpsimd", lambda e, r=r: e.dma_start(
            out=fsumA().rearrange("p (dc rr c) -> p dc rr c", rr=NCORES, c=FS)[:, :, r, :],
            in_=fsag_out[r * D:(r + 1) * D, :].rearrange("(dc p) c -> p dc c", p=128)),
            ins=["fsag_out"], outs=["fsumb"], role=f"g{4 + r % 2}", label=f"fsld{r}")

    def ln_tile_gen(src_key, src, dst, dst_key, tag, par):
        lsc = lnsc_p[par]
        lkey = f"lnsc{par}"
        P.op("vector", lambda e: e.tensor_reduce(lsc[:, 0:1], src[:], AX.X, ALU.add),
             ins=[src_key], outs=[lkey], label=f"lnsum{tag}")
        yield
        P.op("scalar", lambda e: e.activation(dst[:], src[:], AF.Square, accum_out=lsc[:, 1:2]),
             ins=[src_key], outs=[dst_key, lkey], label=f"lnsq{tag}")
        yield
        P.op("vector", lambda e: e.tensor_scalar(lsc[:, 2:3], lsc[:, 0:1], 1.0 / D, None, ALU.mult),
             ins=[lkey], outs=[lkey], label="lnm")
        P.op("vector", lambda e: e.tensor_tensor(lsc[:, 3:4], lsc[:, 2:3], lsc[:, 2:3], ALU.mult),
             ins=[lkey], outs=[lkey], label="lnm2")
        P.op("vector", lambda e: e.tensor_scalar(lsc[:, 4:5], lsc[:, 1:2], 1.0 / D, lsc[:, 3:4],
                                                 ALU.mult, ALU.subtract),
             ins=[lkey], outs=[lkey], label="lnvar")
        yield
        P.op("scalar", lambda e: e.activation(lsc[:, 5:6], lsc[:, 4:5], AF.Ln, bias=epsc[:, 0:1]),
             ins=[lkey, "epsc"], outs=[lkey], label="lnrstd")
        P.op("scalar", lambda e: e.activation(lsc[:, 5:6], lsc[:, 5:6], AF.Exp, scale=-0.5),
             ins=[lkey], outs=[lkey], label="lnrstd2")
        yield
        P.op("vector", lambda e: e.tensor_scalar(lsc[:, 6:7], lsc[:, 2:3], -1.0, lsc[:, 5:6],
                                                 ALU.mult, ALU.mult),
             ins=[lkey], outs=[lkey], label="lnmr")
        yield
        P.op("vector", lambda e: e.tensor_scalar(dst[:], src[:], lsc[:, 5:6], lsc[:, 6:7],
                                                 ALU.mult, ALU.add),
             ins=[src_key, lkey], outs=[dst_key], label=f"lnap{tag}")
        yield

    def transpose_gen(src_ap, src_key, dst, dst_key, n, bank, par):
        for g in range(0, n, 4):
            cnt = min(4, n - g)
            for i2 in range(cnt):
                P.op("tensor", lambda e, g=g, i2=i2: e.transpose(
                    bkb[bank][:, i2 * 128:(i2 + 1) * 128], src_ap[:, (g + i2) * 128:(g + i2 + 1) * 128],
                    idbf[:]),
                    ins=[src_key, "idbf"], outs=[f"bk{bank}"], inc=(i2 == cnt - 1), label="tp")
            eng = "scalar" if (g // 4) % 2 == 0 else "vector"
            if eng == "scalar":
                P.op(eng, lambda e, g=g, cnt=cnt: e.copy(
                    dst[:, g * 128:(g + cnt) * 128], bkb[bank][:, 0:cnt * 128]),
                    ins=[f"bk{bank}"], outs=[dst_key], label="tpev")
            else:
                P.op(eng, lambda e, g=g, cnt=cnt: e.tensor_copy(
                    dst[:, g * 128:(g + cnt) * 128], bkb[bank][:, 0:cnt * 128]),
                    ins=[f"bk{bank}"], outs=[dst_key], label="tpev")
            yield

    ne = (D + 511) // 512

    def ffnA_gen(t):
        par = t % 2
        xb, xkey = (x0t, "x0A") if par == 0 else (x0t2, "x0B")
        hb, hkey2 = hbf_p[par], f"hbfp{par}"
        ht, htkey = htT_p[par], f"htTp{par}"
        xo, xokey = x1t_p[par], f"x1tp{par}"
        base = 4 * par
        P.dma("sync", lambda e, t=t, xb=xb: e.dma_start(out=xb[:], in_=xsh[t * 128:(t + 1) * 128, :]),
              outs=[xkey], role=f"x{par}", label=f"x0ld{t}")
        yield
        yield from ln_tile_gen(xkey, xb, hb, hkey2, f"A{t}", par)
        yield from transpose_gen(hb, hkey2, ht, htkey, DC, base + 3, par)
        for eb in range(ne):
            w = min(512, D - eb * 512)
            for dcb in range(DC):
                P.op("tensor", lambda e, eb=eb, w=w, dcb=dcb, ht=ht: e.matmul(
                    bk[base + eb][:, 0:w], ht[:, dcb * 128:(dcb + 1) * 128],
                    fsumA()[:, dcb * D + eb * 512: dcb * D + eb * 512 + w],
                    start=(dcb == 0), stop=(dcb == DC - 1)),
                    ins=[htkey, "fsumb"], outs=[f"bk{base + eb}"], inc=(dcb == DC - 1), label="ffn1mm")
            yield
        beng = "gpsimd" if t < NT // H2P else "vector"
        P.op(beng, lambda e, xb=xb: e.tensor_add(xb[:], xb[:], biasb[:]),
             ins=[xkey, "biasb"], outs=[xkey], label="ffn1bias")
        yield
        for eb in range(ne):
            w = min(512, D - eb * 512)
            P.op("vector", lambda e, eb=eb, w=w, xb=xb, xo=xo: e.tensor_add(
                xo[:, eb * 512:eb * 512 + w], bk[base + eb][:, 0:w], xb[:, eb * 512:eb * 512 + w]),
                ins=[f"bk{base + eb}", xkey], outs=[xokey], label="ffn1res")
            yield
        P.dma("sync", lambda e, t=t, xo=xo: e.dma_start(out=x1sp[t * 128:(t + 1) * 128, :], in_=xo[:]),
              ins=[xokey], outs=[f"x1sp{t}"], role="xs", label=f"x1sp{t}")
        yield
        yield from ln_tile_gen(xokey, xo, hb, hkey2, f"A2{t}", par)
        hp = t // (NT // H2P)
        hoff = (t % (NT // H2P)) * 128
        P.dma("sync", lambda e, hb=hb, hp=hp, hoff=hoff: e.dma_start(
            out=h2ag_ins[hp][hoff:hoff + 128, :], in_=hb[:]),
            ins=[hkey2], outs=[f"h2w{t}"], role="hw", label=f"h2w{t}")
        yield

    for hp_ in range(H2P):
        lo, hi = hp_ * (NT // H2P), (hp_ + 1) * (NT // H2P)
        for t0_ in range(lo, hi, 2):
            drive2 = [ffnA_gen(t0_), ffnA_gen(t0_ + 1)] if t0_ + 1 < hi else [ffnA_gen(t0_)]
            while drive2:
                keep = []
                for g in drive2:
                    try:
                        next(g)
                        keep.append(g)
                    except StopIteration:
                        pass
                drive2 = keep
        P.op("gpsimd", lambda e, hp_=hp_: e.collective_compute(
            "AllGather", ALU.bypass, rg, [h2ag_ins[hp_]], [h2ag_outs[hp_]]),
            ins=[f"h2w{t}" for t in range(lo, hi)], outs=[f"h2ag{hp_}"], label=f"AGh2{hp_}")

    P.hard_barrier("A2B")

    for cc in range(4):
        for j in range(4):
            P.op("vector", lambda e, cc=cc, j=j: e.tensor_scalar(
                convd()[:, (cc * 4 + j) * 128:(cc * 4 + j + 1) * 128], idbf[:],
                convw[:, cc * 4 + j:cc * 4 + j + 1], None, ALU.mult),
                ins=["idbf", "convw"], outs=["convd"], label="convd")
    for cc in range(4):
        for s01 in range(2):
            P.op("vector", lambda e, cc=cc: e.memset(stagW(cc)[:, 0:4], 0.0),
                 ins=[], outs=[f"stail{cc}_{s01}"], label="stail0") if s01 == 0 else None
    tails = sb("tails", [128, 4 * 2 * 4], BF16)
    for cc in range(4):
        for s01 in range(2):
            P.op("vector", lambda e, cc=cc, s01=s01: e.memset(
                tails[:, (cc * 2 + s01) * 4:(cc * 2 + s01) * 4 + 4], 0.0),
                ins=[], outs=[f"tail{cc}_{s01}"], label="tail0")
    for i in range(4):
        P.op("vector", lambda e, i=i: e.memset(cols[i][:], 0.0), ins=[], outs=[f"cols{i}_0", f"cols{i}_1"], label="cols0")
        P.op("vector", lambda e, i=i: e.memset(S_f[i][:], 0.0), ins=[], outs=[f"Sf{i}"], label="S0")
        P.op("vector", lambda e, i=i: e.memset(S_b[i][:], 0.0), ins=[], outs=[f"Sb{i}"], label="S0b")

    _dbgn = [0]

    def dump(slot, ap_fn, key, w=128, f32=False):
        if not dbg:
            return
        P.op("vector", lambda e: e.tensor_copy(dbgsb[:, slot * 128: slot * 128 + w], ap_fn()),
             ins=[key], outs=["dbgsb"], label=f"dump{slot}")

    def proj_block(pair, s01):
        rb = pair + 4 * s01
        qkey = f"qkvT{s01}"
        zkey = f"zsT{s01}"
        zT = None
        for tc in range(NTCH):
            hb = h2tb((tc + pair * NTCH) % 2)
            hkey = f"h2tb{(tc + pair * NTCH) % 2}"
            t0 = tc * TCH
            hp = t0 // HSEG
            hrow = rb * HSEG + (t0 % HSEG)
            for dcb in range(DC):
                P.dma_pool("sync", lambda e, hp=hp, hrow=hrow, dcb=dcb, hb=hb: e.dma_start(
                    out=hb[:, dcb * TCH:(dcb + 1) * TCH],
                    in_=h2ag_outs[hp][hrow: hrow + TCH, dcb * 128:(dcb + 1) * 128],
                    transpose=True),
                    ins=[f"h2ag{hp}"], outs=[hkey], pool="ht", npool=4, label=f"h2r{rb}_{tc}_{dcb}")
            for cc in range(4):
                ckk = f"sw{cc}"
                pb = cc % 2
                P.op("vector", lambda e, cc=cc, s01=s01: e.tensor_copy(
                    stagW(cc)[:, 0:4], tails[:, (cc * 2 + s01) * 4:(cc * 2 + s01) * 4 + 4]),
                    ins=[f"tail{cc}_{s01}"], outs=[ckk], label="tin")
                for dcb in range(DC):
                    P.op("tensor", lambda e, cc=cc, dcb=dcb, hb=hb, pb=pb: e.matmul(
                        bk[pb][:, 0:TCH],
                        wqkvb[:, dcb * 512 + cc * 128: dcb * 512 + (cc + 1) * 128],
                        hb[:, dcb * TCH:(dcb + 1) * TCH],
                        start=(dcb == 0), stop=(dcb == DC - 1)),
                        ins=["wqkvb", hkey], outs=[f"bk{pb}"], inc=(dcb == DC - 1), label="qkvmm")
                P.op("scalar", lambda e, cc=cc, pb=pb: e.activation(stagW(cc)[:, 4:4 + TCH], bk[pb][:, 0:TCH], AF.Copy),
                     ins=[f"bk{pb}"], outs=[ckk], label="mixev")
                P.op("vector", lambda e, cc=cc, s01=s01: e.tensor_copy(
                    tails[:, (cc * 2 + s01) * 4:(cc * 2 + s01) * 4 + 4], stagW(cc)[:, TCH:TCH + 4]),
                    ins=[ckk], outs=[f"tail{cc}_{s01}"], label="tout")
                for j in range(4):
                    P.op("tensor", lambda e, cc=cc, j=j, pb=pb: e.matmul(
                        bk[pb][:, 0:TCH], convd()[:, (cc * 4 + j) * 128:(cc * 4 + j + 1) * 128],
                        stagW(cc)[:, 1 + j: 1 + j + TCH],
                        start=(j == 0), stop=(j == 3)),
                        ins=["convd", ckk], outs=[f"bk{pb}"], inc=(j == 3), label="convmm")
                P.op("scalar", lambda e, cc=cc, t0=t0, pb=pb, s01=s01: e.activation(
                    qkvT()[:, (cc * 2 + s01) * SEG + t0: (cc * 2 + s01) * SEG + t0 + TCH],
                    bk[pb][:, 0:TCH], AF.Silu),
                    ins=[f"bk{pb}"], outs=[qkey], label="qsilu")
            for cc in range(2):
                for dcb in range(DC):
                    P.op("tensor", lambda e, cc=cc, dcb=dcb, hb=hb: e.matmul(
                        bk[0][:, 0:TCH],
                        wzb[:, dcb * 256 + cc * 128: dcb * 256 + (cc + 1) * 128],
                        hb[:, dcb * TCH:(dcb + 1) * TCH],
                        start=(dcb == 0), stop=(dcb == DC - 1)),
                        ins=["wzb", hkey], outs=["bk0"], inc=(dcb == DC - 1), label="zmm")
                P.op("scalar", lambda e, cc=cc, t0=t0, s01=s01: e.activation(
                    zsT()[:, (cc * 2 + s01) * SEG + t0: (cc * 2 + s01) * SEG + t0 + TCH],
                    bk[0][:, 0:TCH], AF.Silu),
                    ins=["bk0"], outs=[zkey], label="zsilu")
            for dcb in range(DC):
                P.op("tensor", lambda e, dcb=dcb, hb=hb: e.matmul(
                    bk[1][0:4, 0:TCH], wabb[:, dcb * 4:(dcb + 1) * 4],
                    hb[:, dcb * TCH:(dcb + 1) * TCH],
                    start=(dcb == 0), stop=(dcb == DC - 1)),
                    ins=["wabb", hkey], outs=["bk1"], inc=(dcb == DC - 1), label="bgmm")
            P.op("vector", lambda e: e.tensor_copy(bgs[0:4, 0:TCH], bk[1][0:4, 0:TCH]),
                 ins=["bk1"], outs=["bgs"], label="bgst")
            ntt = TCH // 128
            for k in range(ntt):
                gtile = rb * NCK + (t0 // 128) + k
                P.op("tensor", lambda e, k=k, gtile=gtile: e.transpose(
                    bk[1][:, 128 + k * 4: 128 + (k + 1) * 4], bgs[0:4, k * 128:(k + 1) * 128], idf[0:4, 0:4]),
                    ins=["bgs", "idf"], outs=["bk1"], inc=(k == ntt - 1), label="bgT")
            P.op("vector", lambda e, t0=t0, rb=rb, ntt=ntt: e.tensor_copy(
                gpk[:, (rb * NCK + t0 // 128) * 4: (rb * NCK + t0 // 128 + ntt) * 4],
                bk[1][:, 128:128 + ntt * 4]),
                ins=["bk1"], outs=["gpk"], label="bgTev")

    def bg_acts(pair):
        gv = gpk[:].rearrange("p (k c) -> p k c", c=4)
        for s01 in range(2):
            rb = pair + 4 * s01
            k0 = rb * NCK
            P.op("scalar", lambda e, k0=k0: e.activation(
                gv[:, k0:k0 + NCK, 0:2], gv[:, k0:k0 + NCK, 0:2], AF.Sigmoid),
                ins=["gpk"], outs=["gpk"], label="bsig")
            for h in range(2):
                P.op("vector", lambda e, k0=k0, h=h: e.tensor_scalar(
                    gv[:, k0:k0 + NCK, 2 + h:3 + h], gv[:, k0:k0 + NCK, 2 + h:3 + h],
                    dtbb[:, h:h + 1], None, ALU.add),
                    ins=["gpk", "dtbb"], outs=["gpk"], label="gdtb")
                P.op("scalar", lambda e, k0=k0, h=h: e.activation(
                    gtmp[:, 0:NCK], gv[:, k0:k0 + NCK, 2 + h:3 + h], AF.Abs),
                    ins=["gpk"], outs=["gtmp"], label="gabs")
                P.op("scalar", lambda e: e.activation(gtmp[:, 0:NCK], gtmp[:, 0:NCK], AF.Exp, scale=-1.0),
                     ins=["gtmp"], outs=["gtmp"], label="gexp")
                P.op("scalar", lambda e: e.activation(gtmp[:, 0:NCK], gtmp[:, 0:NCK], AF.Ln, bias=epsc[:, 2:3]),
                     ins=["gtmp", "epsc"], outs=["gtmp"], label="gln")
                P.op("vector", lambda e, k0=k0, h=h: e.tensor_relu(
                    gtmp[:, 64:64 + NCK], gv[:, k0:k0 + NCK, 2 + h:3 + h]),
                    ins=["gpk"], outs=["gtmp"], label="grelu")
                P.op("vector", lambda e: e.tensor_add(gtmp[:, 0:NCK], gtmp[:, 0:NCK], gtmp[:, 64:64 + NCK]),
                     ins=["gtmp"], outs=["gtmp"], label="gadd")
                P.op("vector", lambda e, k0=k0, h=h: e.tensor_scalar(
                    gv[:, k0:k0 + NCK, 2 + h:3 + h], gtmp[:, 0:NCK],
                    nalogb[:, h:h + 1], None, ALU.mult),
                    ins=["gtmp", "nalogb"], outs=["gpk"], label="gsc")

    GC, NGC, BB, BETA, SS, SQ, S2, BS, MBBS2, MBS2, BC, KKD, QQD, M2, OSC, BCB = range(16)

    def head_chunk(pair, s01, ck, vh, par):
        rb = pair + 4 * s01
        gtile = rb * NCK + ck
        i = s01 * 2 + vh
        HB = bk[s01 * 2 + vh]
        HBk = f"bk{s01 * 2 + vh}"
        GB = bk[s01 * 2]
        GBk = f"bk{s01 * 2}"
        qkey = f"qkvT{s01}"
        Ktp = lambda: qkvT()[:, (1 * 2 + s01) * SEG + ck * 128: (1 * 2 + s01) * SEG + (ck + 1) * 128]
        Qtp = lambda: qkvT()[:, (0 * 2 + s01) * SEG + ck * 128: (0 * 2 + s01) * SEG + (ck + 1) * 128]
        Vtp = lambda: qkvT()[:, ((2 + vh) * 2 + s01) * SEG + ck * 128: ((2 + vh) * 2 + s01) * SEG + (ck + 1) * 128]
        col = cols[i][:, 16 * par:16 * par + 16]
        ckey = f"cols{i}_{par}"
        gv = gpk[:].rearrange("p (k c) -> p k c", c=4)
        ish = s01 * 2
        if vh == 0:
            P.op("tensor", lambda e: e.matmul(GB[:, 0:128], Ktp(), Ktp(), start=True, stop=True),
                 ins=[qkey], outs=[GBk], inc=False, label="kk")
            P.op("tensor", lambda e: e.matmul(GB[:, 128:256], Ktp(), Qtp(), start=True, stop=True),
                 ins=[qkey], outs=[GBk], inc=False, label="kq")
            P.op("tensor", lambda e: e.matmul(GB[:, 256:384], Qtp(), Qtp(), start=True, stop=True),
                 ins=[qkey], outs=[GBk], label="qq")
            yield
            P.op("vector", lambda e: e.tensor_mul(E1[ish][:], GB[:, 0:128], idf[:]),
                 ins=[GBk, "idf"], outs=[f"E1_{ish}"], label="kkdm")
            yield
            P.op("vector", lambda e: e.tensor_mul(E2[ish][:], GB[:, 256:384], idf[:]),
                 ins=[GBk, "idf"], outs=[f"E2_{ish}"], label="qqdm")
            yield
            yield
            P.op("vector", lambda e: e.tensor_reduce(cols[ish][:, 16 * par + KKD:16 * par + KKD + 1],
                                                     E1[ish][:], AX.X, ALU.add),
                 ins=[f"E1_{ish}"], outs=[f"cols{ish}_{par}"], label="kkdred")
            yield
            P.op("vector", lambda e: e.tensor_reduce(cols[ish][:, 16 * par + QQD:16 * par + QQD + 1],
                                                     E2[ish][:], AX.X, ALU.add),
                 ins=[f"E2_{ish}"], outs=[f"cols{ish}_{par}"], label="qqdred")
            yield
            P.op("scalar", lambda e: e.activation(cols[ish][:, 16 * par + KKD:16 * par + QQD + 1],
                                                  cols[ish][:, 16 * par + KKD:16 * par + QQD + 1],
                                                  AF.Ln, bias=epsc[:, 1:2]),
                 ins=[f"cols{ish}_{par}", "epsc"], outs=[f"cols{ish}_{par}"], label="sln")
            yield
            for ii in (ish, ish + 1):
                P.op("scalar", lambda e, ii=ii: e.activation(
                    cols[ii][:, 16 * par + SS:16 * par + SQ + 1],
                    cols[ish][:, 16 * par + KKD:16 * par + QQD + 1], AF.Exp, scale=-0.5),
                    ins=[f"cols{ish}_{par}"], outs=[f"cols{ii}_{par}"], label="s")
                yield
            yield
            P.op("vector", lambda e: e.tensor_copy(gd2[s01][:], gv[:, gtile, 2:4]),
                 ins=["gpk"], outs=[f"gd2{s01}"], label="gdense")
            yield
            P.op("tensor", lambda e: e.transpose(GB[0:2, 384:512], gd2[s01][:], idf[:]),
                 ins=[f"gd2{s01}", "idf"], outs=[GBk], label="gT")
            yield
            P.op("vector", lambda e: e.tensor_copy(grow2[s01][:], GB[0:2, 384:512]),
                 ins=[GBk], outs=[f"grow2{s01}"], label="growev")
            yield
            P.op("vector", lambda e: e.tensor_tensor_scan(gcs2[s01][:], grow2[s01][:], ones2[:], 0.0,
                                                          ALU.add, ALU.mult),
                 ins=[f"grow2{s01}", "ones2"], outs=[f"gcs2{s01}"], label="gscan")
            yield
            P.op("tensor", lambda e: e.transpose(GB[:, 384:386], gcs2[s01][:], idf[0:2, 0:2]),
                 ins=[f"gcs2{s01}", "idf"], outs=[GBk], label="gcT")
            yield
            for ii in (ish, ish + 1):
                P.op("vector", lambda e, ii=ii: e.tensor_copy(
                    cols[ii][:, 16 * par + GC:16 * par + GC + 1],
                    GB[:, 384 + (ii - ish):385 + (ii - ish)]),
                    ins=[GBk], outs=[f"cols{ii}_{par}"], label="gcev")
                yield
            yield "shared_done"
        P.op("vector", lambda e: e.tensor_scalar(col[:, NGC:NGC + 1], col[:, GC:GC + 1], -1.0, None, ALU.mult),
             ins=[ckey], outs=[ckey], label="ngc")
        yield
        P.op("scalar", lambda e: e.activation(col[:, BB:BB + 1], col[:, GC:GC + 1], AF.Exp),
             ins=[ckey], outs=[ckey], label="bexp")
        yield
        P.op("vector", lambda e: e.tensor_copy(col[:, BETA:BETA + 1], gv[:, gtile, vh:vh + 1]),
             ins=["gpk"], outs=[ckey], label="beta")
        yield
        P.op("vector", lambda e: e.tensor_tensor(col[:, S2:S2 + 1], col[:, SS:SS + 1], col[:, SS:SS + 1], ALU.mult),
             ins=[ckey], outs=[ckey], label="s2")
        yield
        P.op("vector", lambda e: e.tensor_tensor(col[:, BS:BS + 1], col[:, BETA:BETA + 1], col[:, SS:SS + 1], ALU.mult),
             ins=[ckey], outs=[ckey], label="bs")
        yield
        P.op("vector", lambda e: e.tensor_scalar(col[:, MBS2:MBS2 + 1], col[:, S2:S2 + 1], -1.0,
                                                 col[:, BETA:BETA + 1], ALU.mult, ALU.mult),
             ins=[ckey], outs=[ckey], label="mbs2")
        yield
        P.op("vector", lambda e: e.tensor_tensor(col[:, MBBS2:MBBS2 + 1], col[:, MBS2:MBS2 + 1],
                                                 col[:, BB:BB + 1], ALU.mult),
             ins=[ckey], outs=[ckey], label="mbbs2")
        yield
        yield
        P.op("vector", lambda e: e.tensor_scalar(E1[i][:], ones128[:], col[:, GC:GC + 1], None, ALU.mult),
             ins=["ones128", ckey], outs=[f"E1_{i}"], label="colb")
        yield
        P.op("tensor", lambda e: e.transpose(HB[:, 256:384], E1[i][:], idf[:]),
             ins=[f"E1_{i}", "idf"], outs=[HBk], label="gcbT")
        yield
        P.op("vector", lambda e: e.tensor_add(E1[i][:], HB[:, 256:384], masku[:]),
             ins=[HBk, "masku"], outs=[f"E1_{i}"], label="e1a")
        yield
        P.op("vector", lambda e: e.tensor_add(E2[i][:], HB[:, 256:384], maskl[:]),
             ins=[HBk, "maskl"], outs=[f"E2_{i}"], label="e2a")
        yield
        P.op("scalar", lambda e: e.activation(E1[i][:], E1[i][:], AF.Exp, bias=col[:, GC:GC + 1], scale=-1.0),
             ins=[f"E1_{i}", ckey], outs=[f"E1_{i}"], label="E1")
        yield
        P.op("scalar", lambda e: e.activation(E2[i][:], E2[i][:], AF.Exp, bias=col[:, NGC:NGC + 1], scale=1.0),
             ins=[f"E2_{i}", ckey], outs=[f"E2_{i}"], label="E2")
        yield
        yield
        P.op("vector", lambda e: e.tensor_copy(col[:, BCB:BCB + 1], E2[i][:, 127:128]),
             ins=[f"E2_{i}"], outs=[ckey], label="bCb")
        yield
        P.op("vector", lambda e: e.tensor_tensor(col[:, BC:BC + 1], col[:, BCB:BCB + 1], col[:, BB:BB + 1], ALU.mult),
             ins=[ckey], outs=[ckey], label="bC")
        yield
        P.op("vector", lambda e: e.scalar_tensor_tensor(Xb[i][:], GB[:, 0:128], col[:, MBS2:MBS2 + 1],
                                                        E1[i][:], ALU.mult, ALU.mult),
             ins=[GBk, ckey, f"E1_{i}"], outs=[f"Xb{i}"], label="X")
        yield
        P.op("vector", lambda e, par=par: e.tensor_mul(Gb[i][par][:], GB[:, 128:256], E2[i][:]),
             ins=[GBk, f"E2_{i}"], outs=[f"Gb{i}_{par}"], label="G")
        yield
        yield
        P.op("tensor", lambda e: e.transpose(bkb[s01 * 2 + vh][:, 768:896], Xb[i][:], idbf[:]),
             ins=[f"Xb{i}", "idbf"], outs=[HBk], label="XT")
        yield
        P.op("scalar", lambda e: e.activation(Yb[i][:], bkb[s01 * 2 + vh][:, 768:896], AF.Copy),
             ins=[HBk], outs=[f"Yb{i}"], label="YTev")
        yield
        P.op("tensor", lambda e: e.transpose(bkb[s01 * 2 + vh][:, 768:896], Vtp(), idbf[:]),
             ins=[qkey, "idbf"], outs=[HBk], label="VT")
        yield
        P.op("vector", lambda e, par=par: e.tensor_scalar(Vbs[i][par][:], bkb[s01 * 2 + vh][:, 768:896],
                                                          col[:, BS:BS + 1], None, ALU.mult),
             ins=[HBk, ckey], outs=[f"Vbs{i}_{par}"], label="Vbs")
        yield
        P.op("tensor", lambda e: e.transpose(bkb[s01 * 2 + vh][:, 768:896], Ktp(), idbf[:]),
             ins=[qkey, "idbf"], outs=[HBk], label="KT")
        yield
        P.op("scalar", lambda e, par=par: e.activation(Ktok[i][par][:], bkb[s01 * 2 + vh][:, 768:896], AF.Copy),
             ins=[HBk], outs=[f"Ktok{i}_{par}"], label="KTev")
        yield
        yield
        P.op("vector", lambda e: e.tensor_add(PYa[i][:, 0:128], idbf[:], Yb[i][:]),
             ins=["idbf", f"Yb{i}"], outs=[f"PYa{i}"], label="P2")
        yield
        P.op("tensor", lambda e: e.matmul(HB[:, 384:512], Xb[i][:], Yb[i][:], start=True, stop=True),
             ins=[f"Xb{i}", f"Yb{i}"], outs=[HBk], label="Y2mm")
        yield
        P.op("scalar", lambda e: e.activation(PYa[i][:, 128:256], HB[:, 384:512], AF.Copy),
             ins=[HBk], outs=[f"PYa{i}y"], label="Y2ev")
        yield
        P.op("tensor", lambda e: e.matmul(HB[:, 384:512], Yb[i][:], Xb[i][:], start=True, stop=True),
             ins=[f"Yb{i}", f"Xb{i}"], outs=[HBk], label="X2mm")
        yield
        P.op("vector", lambda e: e.tensor_copy(Xs[i][:], HB[:, 384:512]),
             ins=[HBk], outs=[f"Xs{i}"], label="X2ev")
        yield
        yield
        xcur, xck = Xs[i], f"Xs{i}"
        xoth, xok = Xb[i], f"Xb{i}"
        pycur, pyck = PYa[i], f"PYa{i}"
        pyoth, pyok = PYb[i], f"PYb{i}"
        for stp in range(5):
            m = 2 << stp
            P.op("tensor", lambda e, xcur=xcur, pycur=pycur, xck=xck, pyck=pyck: e.matmul(
                HB[:, 0:256], xcur[:], pycur[:], start=True, stop=False),
                ins=[xck, pyck, pyck + "y"], outs=[HBk], inc=False, label=f"dbl{m}")
            P.op("tensor", lambda e, pycur=pycur, pyck=pyck: e.matmul(
                HB[:, 0:128], idbf[:], pycur[:, 0:128], start=False, stop=True),
                ins=["idbf", pyck], outs=[HBk], label=f"dblP{m}")
            yield
            eng = "vector" if stp % 2 == 0 else "scalar"
            if eng == "vector":
                P.op("vector", lambda e, pyoth=pyoth: e.tensor_copy(
                    pyoth[:], HB[:, 0:256]),
                    ins=[HBk], outs=[pyok, pyok + "y"], label=f"Pup{m}")
            else:
                P.op("scalar", lambda e, pyoth=pyoth: e.copy(
                    pyoth[:], HB[:, 0:256]),
                    ins=[HBk], outs=[pyok, pyok + "y"], label=f"Pup{m}")
            yield
            if stp < 4:
                P.op("tensor", lambda e, xcur=xcur, pycur=pycur, xck=xck, pyck=pyck: e.matmul(
                    HB[:, 256:384], pycur[:, 128:256], xcur[:], start=True, stop=True),
                    ins=[pyck + "y", xck], outs=[HBk], label=f"Xsq{m}")
                yield
                P.op("vector", lambda e, xoth=xoth, xok=xok: e.tensor_copy(xoth[:], HB[:, 256:384]),
                     ins=[HBk], outs=[xok], label=f"Xev{m}")
                yield
            yield
            xcur, xck, xoth, xok = xoth, xok, xcur, xck
            pycur, pyck, pyoth, pyok = pyoth, pyok, pycur, pyck
        P.op("tensor", lambda e, pycur=pycur, pyck=pyck: e.transpose(
            bkb[s01 * 2 + vh][:, 768:896], pycur[:, 128:256], idbf[:]),
            ins=[pyck + "y", "idbf"], outs=[HBk], label="Y64T")
        yield
        P.op("scalar", lambda e: e.activation(Xs[i][:], bkb[s01 * 2 + vh][:, 768:896], AF.Copy),
             ins=[HBk], outs=[f"Xs{i}"], label="Y64Tev")
        yield
        P.op("tensor", lambda e, pycur=pycur, pyck=pyck: e.matmul(
            HB[:, 256:384], Xs[i][:], pycur[:, 0:128], start=True, stop=False),
            ins=[f"Xs{i}", pyck], outs=[HBk], inc=False, label="Pfmm")
        P.op("tensor", lambda e, pycur=pycur, pyck=pyck: e.matmul(
            HB[:, 256:384], idbf[:], pycur[:, 0:128], start=False, stop=True),
            ins=["idbf", pyck], outs=[HBk], label="PfmmI")
        yield
        P.op("vector", lambda e, par=par: e.tensor_copy(Pf[i][par][:], HB[:, 256:384]),
             ins=[HBk], outs=[f"Pf{i}_{par}"], label="Pfin")
        yield

    def tail_chunk(pair, s01, ck, vh, par):
        i = s01 * 2 + vh
        TB = bk[4 + i]
        TBk = f"bk{4 + i}"
        qkey = f"qkvT{s01}"
        zkey = f"zsT{s01}"
        Ktp = lambda: qkvT()[:, (1 * 2 + s01) * SEG + ck * 128: (1 * 2 + s01) * SEG + (ck + 1) * 128]
        Qtp = lambda: qkvT()[:, (0 * 2 + s01) * SEG + ck * 128: (0 * 2 + s01) * SEG + (ck + 1) * 128]
        col = cols[i][:, 16 * par:16 * par + 16]
        ckey = f"cols{i}_{par}"
        P.op("tensor", lambda e: e.matmul(TB[:, 128:256], Ktp(), S_b[i][:], start=True, stop=True),
             ins=[qkey, f"Sb{i}"], outs=[TBk], label="KS0")
        yield
        P.op("vector", lambda e, par=par: e.scalar_tensor_tensor(Rb[i][:], TB[:, 128:256],
                                                                 col[:, MBBS2:MBBS2 + 1],
                                                                 Vbs[i][par][:], ALU.mult, ALU.add),
             ins=[TBk, ckey, f"Vbs{i}_{par}"], outs=[f"Rb{i}"], label="R")
        yield
        yield
        P.op("tensor", lambda e, par=par: e.matmul(TB[:, 128:256], Pf[i][par][:], Rb[i][:],
                                                   start=True, stop=True),
             ins=[f"Pf{i}_{par}", f"Rb{i}"], outs=[TBk], label="umm")
        yield
        P.op("scalar", lambda e: e.activation(Ub[i][:], TB[:, 128:256], AF.Copy),
             ins=[TBk], outs=[f"Ub{i}"], label="uev")
        yield
        P.op("vector", lambda e: e.tensor_scalar(Upb[i][:], Ub[i][:], col[:, BCB:BCB + 1], None, ALU.mult),
             ins=[f"Ub{i}", ckey], outs=[f"Upb{i}"], label="upev")
        yield
        yield
        P.op("tensor", lambda e, par=par: e.matmul(TB[:, 128:256], Gb[i][par][:], Ub[i][:],
                                                   start=True, stop=True),
             ins=[f"Gb{i}_{par}", f"Ub{i}"], outs=[TBk], inc=False, label="Gu")
        P.op("tensor", lambda e: e.matmul(TB[:, 384:512], Qtp(), S_b[i][:], start=True, stop=True),
             ins=[qkey, f"Sb{i}"], outs=[TBk], label="QS0")
        yield
        P.op("vector", lambda e: e.tensor_scalar(Otmp[i][:], TB[:, 384:512], col[:, BB:BB + 1], None, ALU.mult),
             ins=[TBk, ckey], outs=[f"Otmp{i}"], label="Oqs")
        yield
        P.op("vector", lambda e: e.tensor_add(Of[i][:], Otmp[i][:], TB[:, 128:256]),
             ins=[TBk, f"Otmp{i}"], outs=[f"Of{i}"], label="O")
        yield
        yield
        P.op("tensor", lambda e, par=par: e.matmul(TB[:, 128:256], Ktok[i][par][:], Upb[i][:],
                                                   start=True, stop=True),
             ins=[f"Ktok{i}_{par}", f"Upb{i}"], outs=[TBk], label="Smm")
        yield
        P.op("vector", lambda e: e.scalar_tensor_tensor(S_f[i][:], S_f[i][:], col[:, BC:BC + 1],
                                                        TB[:, 128:256], ALU.mult, ALU.add),
             ins=[TBk, ckey, f"Sf{i}"], outs=[f"Sf{i}"], label="Sup")
        yield
        P.op("scalar", lambda e: e.activation(S_b[i][:], S_f[i][:], AF.Copy),
             ins=[f"Sf{i}"], outs=[f"Sb{i}"], label="Sbcp")
        yield
        yield
        P.op("scalar", lambda e: e.activation(Otmp[i][:], Of[i][:], AF.Square,
                                              accum_out=col[:, M2:M2 + 1]),
             ins=[f"Of{i}"], outs=[f"Otmp{i}", ckey], label="Osq")
        yield
        P.op("vector", lambda e: e.tensor_tensor(col[:, OSC:OSC + 1], col[:, SQ:SQ + 1], col[:, SQ:SQ + 1], ALU.mult),
             ins=[ckey], outs=[ckey], label="sq2")
        yield
        P.op("vector", lambda e: e.tensor_scalar(col[:, OSC:OSC + 1], col[:, OSC:OSC + 1], 1.0 / (128.0 * 128.0),
                                                 col[:, M2:M2 + 1], ALU.mult, ALU.mult),
             ins=[ckey], outs=[ckey], label="m2s")
        yield
        P.op("scalar", lambda e: e.activation(col[:, OSC:OSC + 1], col[:, OSC:OSC + 1],
                                              AF.Ln, bias=epsc[:, 1:2]),
             ins=[ckey, "epsc"], outs=[ckey], label="rstd")
        yield
        P.op("scalar", lambda e: e.activation(col[:, OSC:OSC + 1], col[:, OSC:OSC + 1],
                                              AF.Exp, scale=-0.5),
             ins=[ckey], outs=[ckey], label="rstd2")
        yield
        P.op("vector", lambda e: e.tensor_scalar(col[:, OSC:OSC + 1], col[:, OSC:OSC + 1], DKmh,
                                                 col[:, SQ:SQ + 1], ALU.mult, ALU.mult),
             ins=[ckey], outs=[ckey], label="osc")
        yield
        yield
        P.op("vector", lambda e: e.tensor_scalar(Of[i][:], Of[i][:], col[:, OSC:OSC + 1], None, ALU.mult),
             ins=[f"Of{i}", ckey], outs=[f"Of{i}"], label="on1")
        yield
        P.op("gpsimd", lambda e: e.tensor_mul(Of[i][:], Of[i][:], nwb[:]),
             ins=[f"Of{i}", "nwb"], outs=[f"Of{i}"], label="on2")
        yield
        P.op("tensor", lambda e: e.transpose(TB[:, 256:384], Of[i][:], idf[:]),
             ins=[f"Of{i}", "idf"], outs=[TBk], label="OT")
        yield
        osl = (s01 * 2 + vh) * SEG
        P.op("vector", lambda e, osl=osl: e.tensor_mul(ogsb()[:, osl + ck * 128: osl + (ck + 1) * 128],
                                              TB[:, 256:384],
                                              zsT()[:, (vh * 2 + s01) * SEG + ck * 128: (vh * 2 + s01) * SEG + (ck + 1) * 128]),
             ins=[TBk, zkey], outs=[f"og{s01}_{vh}"], label="og")
        yield

    def drive(gens):
        while gens:
            nxt = []
            for g in gens:
                try:
                    next(g)
                    nxt.append(g)
                except StopIteration:
                    pass
            gens = nxt

    def chain_gen(pair, s01):
        tails = []
        for ck in range(NCK):
            par = ck % 2
            h1 = head_chunk(pair, s01, ck, 1, par)
            sub = [head_chunk(pair, s01, ck, 0, par)] + tails
            h1_started = [False]
            while sub:
                keep = []
                for g in sub:
                    try:
                        v = next(g)
                        keep.append(g)
                        if v == "shared_done" and not h1_started[0]:
                            keep.append(h1)
                            h1_started[0] = True
                    except StopIteration:
                        pass
                sub = keep
                yield
            if not h1_started[0]:
                for _ in h1:
                    pass
            tails = [tail_chunk(pair, s01, ck, 0, par), tail_chunk(pair, s01, ck, 1, par)]
        sub = tails
        while sub:
            keep = []
            for g in sub:
                try:
                    next(g)
                    keep.append(g)
                except StopIteration:
                    pass
            sub = keep
            yield


    for pair in range(NSEG):
        for s01 in range(2):
            proj_block(pair, s01)
        bg_acts(pair)
        gens = [chain_gen(pair, 0), chain_gen(pair, 1)]
        while gens:
            nxt = []
            for g in gens:
                try:
                    next(g)
                    nxt.append(g)
                except StopIteration:
                    pass
            gens = nxt
        for s01 in range(2):
            rb = pair + 4 * s01
            for vh in range(2):
                P.dma_pool("sync", lambda e, s01=s01, rb=rb, vh=vh: e.dma_start(
                    out=ogag_in[vh * 128:(vh + 1) * 128, rb * SEG:(rb + 1) * SEG],
                    in_=ogsb()[:, (s01 * 2 + vh) * SEG:(s01 * 2 + vh + 1) * SEG]),
                    ins=[f"og{s01}_{vh}"], outs=[f"ogw{rb}_{vh}"], pool="og", npool=2,
                    label=f"ogw{rb}_{vh}")

    P.op("gpsimd", lambda e: e.collective_compute("AllGather", ALU.bypass, rg, [ogag_in], [ogag_out]),
         ins=[f"ogw{rb}_{vh}" for rb in range(8) for vh in range(2)], outs=["ogag"], label="AGog")

    P.hard_barrier("B2C")

    P.dma("gpsimd", lambda e: e.dma_start(
        out=woutb().rearrange("p (vc c) -> p vc c", c=D),
        in_=wout_in.rearrange("(vc p) c -> p vc c", p=128)), outs=["woutb"], role="g6", label="wld")
    _selvals = {}

    def selval(e):
        if "v" not in _selvals:
            _selvals["v"] = e.partition_id() * NT
        return _selvals["v"]

    ogt2 = [a1b(ogt2_o, 16 * 128), a1b(ogt2_o + 16 * 128 // 2, 16 * 128)]

    def c1_gen(t):
        par = t % 2
        xb, xkey = (x0t, "x0A") if par == 0 else (x0t2, "x0B")
        og, ogkey = ogt2[par], f"ogtT{par}"
        base = 4 * par
        P.dma_pool("sync", lambda e, t=t, og=og: e.dma_start(
            out=og.rearrange("p (vc c) -> p vc c", c=128),
            in_=ogag_out.rearrange("(vc p) tt -> p vc tt", p=128)[:, :, t * 128:][:, :, bass.ts(selval(e), 128)]),
            ins=["ogag"], outs=[ogkey], pool="oc", npool=2, label=f"ogr{t}")
        P.dma("sync", lambda e, t=t, xb=xb: e.dma_start(out=xb[:], in_=x1sp[t * 128:(t + 1) * 128, :]),
              ins=[f"x1sp{t}"], outs=[xkey], role=f"x{par}", label=f"x1ld{t}")
        yield
        for eb in range(ne):
            w = min(512, D - eb * 512)
            for vc in range(16):
                P.op("tensor", lambda e, vc=vc, w=w, eb=eb, og=og: e.matmul(
                    bk[base + eb][:, 0:w], og[:, vc * 128:(vc + 1) * 128],
                    woutb()[:, vc * D + eb * 512: vc * D + eb * 512 + w],
                    start=(vc == 0), stop=(vc == 15)),
                    ins=[ogkey, "woutb"], outs=[f"bk{base + eb}"], inc=(vc == 15), label="womm")
            yield
        for eb in range(ne):
            w = min(512, D - eb * 512)
            P.op("vector", lambda e, eb=eb, w=w, xb=xb: e.tensor_add(
                xb[:, eb * 512:eb * 512 + w], bk[base + eb][:, 0:w], xb[:, eb * 512:eb * 512 + w]),
                ins=[f"bk{base + eb}", xkey], outs=[xkey], label="x2add")
            yield
        P.dma("sync", lambda e, t=t, xb=xb: e.dma_start(out=x2sp[t * 128:(t + 1) * 128, :], in_=xb[:]),
              ins=[xkey], outs=[f"x2sp{t}"], role="x2w", label=f"x2w{t}")
        yield

    for t0_ in range(0, NT, 2):
        dr = [c1_gen(t0_), c1_gen(t0_ + 1)] if t0_ + 1 < NT else [c1_gen(t0_)]
        while dr:
            keep = []
            for g in dr:
                try:
                    next(g)
                    keep.append(g)
                except StopIteration:
                    pass
            dr = keep

    P.hard_barrier("C12")
    for r in range(NCORES):
        P.dma("gpsimd", lambda e, r=r: e.dma_start(
            out=fsumC().rearrange("p (dc rr c) -> p dc rr c", rr=NCORES, c=FS)[:, :, r, :],
            in_=fsag_out[r * D:(r + 1) * D, :].rearrange("(dc p) c -> p dc c", p=128)),
            ins=["fsag_out"], outs=["fsumb"], role=f"g{4 + r % 2}", label=f"fsld2{r}")

    def c2_gen(t):
        par = t % 2
        xb, xkey = (x0t, "x0A") if par == 0 else (x0t2, "x0B")
        hb, hkey2 = hbf_p[par], f"hbfp{par}"
        ht, htkey = htT_p[par], f"htTp{par}"
        xo, xokey = x1t_p[par], f"x1tp{par}"
        base = 4 * par
        P.dma("sync", lambda e, t=t, xb=xb: e.dma_start(out=xb[:], in_=x2sp[t * 128:(t + 1) * 128, :]),
              ins=[f"x2sp{t}"], outs=[xkey], role=f"x{par}", label=f"x2ld{t}")
        yield
        yield from ln_tile_gen(xkey, xb, hb, hkey2, f"C{t}", par)
        yield from transpose_gen(hb, hkey2, ht, htkey, DC, base + 3, par)
        for eb in range(ne):
            w = min(512, D - eb * 512)
            for dcb in range(DC):
                P.op("tensor", lambda e, eb=eb, w=w, dcb=dcb, ht=ht: e.matmul(
                    bk[base + eb][:, 0:w], ht[:, dcb * 128:(dcb + 1) * 128],
                    fsumC()[:, dcb * D + eb * 512: dcb * D + eb * 512 + w],
                    start=(dcb == 0), stop=(dcb == DC - 1)),
                    ins=[htkey, "fsumb"], outs=[f"bk{base + eb}"], inc=(dcb == DC - 1), label="ffn2mm")
            yield
        P.op("gpsimd", lambda e, xb=xb: e.tensor_add(xb[:], xb[:], biasb[:]),
             ins=[xkey, "biasb"], outs=[xkey], label="ffn2bias")
        yield
        for eb in range(ne):
            w = min(512, D - eb * 512)
            P.op("vector", lambda e, eb=eb, w=w, xb=xb, xo=xo: e.tensor_add(
                xo[:, eb * 512:eb * 512 + w], bk[base + eb][:, 0:w], xb[:, eb * 512:eb * 512 + w]),
                ins=[f"bk{base + eb}", xkey], outs=[xokey], label="ffn2res")
            yield
        P.dma("sync", lambda e, t=t, xo=xo: e.dma_start(out=ysh[t * 128:(t + 1) * 128, :], in_=xo[:]),
              ins=[xokey], outs=[f"y{t}"], role="yo", label=f"yout{t}")
        yield

    for t0_ in range(0, NT, 2):
        dr = [c2_gen(t0_), c2_gen(t0_ + 1)] if t0_ + 1 < NT else [c2_gen(t0_)]
        while dr:
            keep = []
            for g in dr:
                try:
                    next(g)
                    keep.append(g)
                except StopIteration:
                    pass
            dr = keep

    if dbg:
        P.hard_barrier("dbgout")
        P.dma("sync", lambda e: e.dma_start(
            out=dbgt.rearrange("s p c -> p s c"), in_=dbgsb[:].rearrange("p (s c) -> p s c", c=128)),
            ins=["dbgsb"], outs=["dbgdram"], role="dbg", label="dbgout")
    P.final_waits("sync")

    sems = {}
    for name in P.used_sems():
        sems[name] = es.enter_context(nc.semaphore(name))
    P.maxop = maxop
    block = es.enter_context(nc.Block())
    P.replay(nc, block, sems)
    es.close()
    meta = dict(o_h2=o_h2, o_qkvT=o_qkvT, o_zsT=o_zsT, o_stag=o_stag, o_ogsb=o_ogsb,
                o_convd=o_convd, SEG=SEG, NT=NT, NCK=NCK, TCH=TCH, DC=DC, a1w=a1w)
    return nc, len(P.ops), meta


def make_inputs(core, inp, T, D):
    b = core // 4
    SEG = T // 4
    t0 = SEG * (core % 4)
    KEY = 1024
    x = np.ascontiguousarray(np.asarray(inp["L0_attn_hidden_states"])[b, t0:t0 + SEG, :], dtype=np.float32)
    wqkv = np.asarray(inp["L0_attn_W_qkv"])
    qcol = slice(128 * core, 128 * core + 128)
    kcol = slice(KEY + 128 * core, KEY + 128 * core + 128)
    vcol = slice(2 * KEY + 256 * core, 2 * KEY + 256 * core + 256)
    wqkv_s = np.ascontiguousarray(np.concatenate(
        [wqkv[:, qcol], wqkv[:, kcol], wqkv[:, vcol]], axis=1), dtype=np.float32)
    wz_s = np.ascontiguousarray(np.asarray(inp["L0_attn_W_z"])[:, 256 * core:256 * core + 256], dtype=np.float32)
    wb = np.asarray(inp["L0_attn_W_b"])[:, 2 * core:2 * core + 2]
    wa = np.asarray(inp["L0_attn_W_a"])[:, 2 * core:2 * core + 2]
    wab_s = np.ascontiguousarray(np.concatenate([wb, wa], axis=1), dtype=np.float32)
    FS = D // 8
    fact_s = np.ascontiguousarray(np.asarray(inp["L0_ffn_factors"])[:, :, FS * core:FS * core + FS],
                                  dtype=np.float32)
    convw = np.asarray(inp["L0_attn_conv_w"])
    conv_rows = np.concatenate([convw[qcol.start:qcol.stop], convw[kcol.start:kcol.stop],
                                convw[vcol.start:vcol.stop]], axis=0)
    dtb = np.asarray(inp["L0_attn_dt_bias"])[2 * core:2 * core + 2].reshape(1, 2)
    alog = np.asarray(inp["L0_attn_A_log"])[2 * core:2 * core + 2].reshape(1, 2)

    ii = np.arange(128)
    jj = np.arange(128)
    Umat = (ii[:, None] <= jj[None, :]).astype(np.float32)
    masku = (BIG * (jj[None, :] >= ii[:, None])).astype(np.float32)
    maskl = (-BIG * (jj[None, :] < ii[:, None])).astype(np.float32)

    return {
        "xsh": x,
        "wqkv_s": wqkv_s,
        "wz_s": wz_s,
        "wab_s": wab_s,
        "fact_s": fact_s,
        "fbias": np.ascontiguousarray(np.asarray(inp["L0_ffn_bias"]).reshape(1, D), dtype=np.float32),
        "wout_in": np.ascontiguousarray(np.asarray(inp["L0_attn_W_out"]), dtype=np.float32),
        "convw_s": np.ascontiguousarray(conv_rows, dtype=np.float32),
        "dtb_s": np.ascontiguousarray(dtb, dtype=np.float32),
        "alog_s": np.ascontiguousarray(alog, dtype=np.float32),
        "normw": np.ascontiguousarray(np.asarray(inp["L0_attn_norm_weight"]).reshape(1, 128), dtype=np.float32),
        "IDbf": np.eye(128, dtype=ml_dtypes.bfloat16),
        "IDf": np.eye(128, dtype=np.float32),
        "UINC": Umat,
        "MASKU": masku,
        "MASKL": maskl,
        "ONESROW": np.ones((1, 128), dtype=np.float32),
        "ONECOL": np.ones((128, 1), dtype=np.float32),
    }


def kernel(**inputs):
    T, D = 8192, 2048
    nc, _nops, _meta = build_nc(T=T, D=D)
    in_maps = [make_inputs(c, inputs, T, D) for c in range(NCORES)]
    res = run_bass_kernel_spmd(nc, in_maps, core_ids=list(range(NCORES)))
    SEG = T // 4
    out = np.empty((2, T, D), dtype=np.float32)
    for c in range(NCORES):
        b = c // 4
        t0 = SEG * (c % 4)
        out[b, t0:t0 + SEG, :] = res.results[c]["ysh"]
    return out


# version 14 (final)
# speedup vs baseline: 1.0071x; 1.0071x over previous
import math
import numpy as np
import ml_dtypes
import concourse.bass as bass
import concourse.mybir as mybir
from concourse.bass_utils import run_bass_kernel_spmd

F32 = mybir.dt.float32
BF16 = mybir.dt.bfloat16
AF = mybir.ActivationFunctionType
ALU = mybir.AluOpType
AX = mybir.AxisListType

NCORES = 8
BIG = 1.0e9
INTERLEAVE = 1
ENGINES = ("tensor", "vector", "scalar", "gpsimd", "sync")


class _Op:
    __slots__ = ("eng", "emit", "ins", "outs", "inc", "dma_sem", "waits", "event", "label")

    def __init__(self, eng, emit, ins, outs, inc=True, dma_sem=None, label=""):
        self.eng = eng
        self.emit = emit
        self.ins = list(ins)
        self.outs = list(outs)
        self.inc = inc
        self.dma_sem = dma_sem
        self.waits = []
        self.event = None
        self.label = label


class Prog:
    def __init__(self):
        self.ops = []
        self.lw = {}
        self.rd = {}
        self.acq = {e: {} for e in ENGINES}
        self.rel = {}
        self.cnt = {}
        self.pend = {e: [] for e in ENGINES}
        self.pend_reads = {e: [] for e in ENGINES}
        self.pool_idx = {}
        self.esem = {e: f"E{e[0]}" for e in ENGINES}
        for e in ENGINES:
            self.cnt[self.esem[e]] = 0

    def _need(self, eng, ev):
        return self.acq[eng].get(ev[0], 0) < ev[1]

    def _acquire(self, eng, ev):
        a = self.acq[eng]
        for s, v in self.rel.get(ev, {}).items():
            if a.get(s, 0) < v:
                a[s] = v
        if a.get(ev[0], 0) < ev[1]:
            a[ev[0]] = ev[1]

    def _issue(self, op):
        eng = op.eng
        deps = []
        for k in op.ins:
            ev = self.lw.get(k)
            if ev is not None:
                deps.append((ev, k))
        for k in op.outs:
            ev = self.lw.get(k)
            if ev is not None:
                deps.append((ev, k))
            for rev in self.rd.get(k, ()):
                deps.append((rev, k))
        for ev, key in deps:
            if isinstance(ev, tuple) and ev[0] == "pend":
                if ev[1] != eng:
                    raise RuntimeError(f"cross-engine pending dep on {key} ({ev[1]} -> {eng}) at {op.label}")
                continue
            if ev[0].startswith("D"):
                ev = (ev[0], self.cnt[ev[0]])
            if self._need(eng, ev):
                op.waits.append(ev)
                self._acquire(eng, ev)
        if op.dma_sem is not None:
            sem = op.dma_sem
            self.cnt[sem] = self.cnt.get(sem, 0) + 16
            op.event = (sem, self.cnt[sem])
        elif op.inc:
            sem = self.esem[eng]
            self.cnt[sem] += 1
            op.event = (sem, self.cnt[sem])
        if op.event is not None:
            snap = dict(self.acq[eng])
            snap[op.event[0]] = op.event[1]
            self.rel[op.event] = snap
            for k in self.pend[eng]:
                self.lw[k] = op.event
                self.rd[k] = []
            self.pend[eng] = []
            for k in self.pend_reads[eng]:
                self.rd.setdefault(k, []).append(op.event)
            self.pend_reads[eng] = []
            for k in op.outs:
                self.lw[k] = op.event
                self.rd[k] = []
            for k in op.ins:
                self.rd.setdefault(k, []).append(op.event)
        else:
            for k in op.outs:
                self.lw[k] = ("pend", eng)
                self.rd[k] = []
                self.pend[eng].append(k)
            self.pend_reads[eng].extend(op.ins)
        self.ops.append(op)

    def op(self, eng, emit, ins=(), outs=(), inc=True, label=""):
        self._issue(_Op(eng, emit, ins, outs, inc=inc, label=label))

    def dma(self, eng, emit, ins=(), outs=(), role="d", label=""):
        sem = f"D{role}"
        if sem not in self.cnt:
            self.cnt[sem] = 0
        o = _Op(eng, emit, ins, outs, dma_sem=sem, label=label)
        if self.cnt[sem] > 0:
            prev = (sem, self.cnt[sem])
            if self._need(eng, prev):
                o.waits.append(prev)
                self._acquire(eng, prev)
        self._issue(o)

    def dma_pool(self, eng, emit, ins=(), outs=(), pool="p", npool=4, label=""):
        i = self.pool_idx.get(pool, 0)
        self.pool_idx[pool] = (i + 1) % npool
        self.dma(eng, emit, ins, outs, role=f"{pool}{i}", label=label)

    def hard_barrier(self, label="barrier"):
        tot = dict(self.cnt)
        for eng in ENGINES:
            for sem, val in tot.items():
                if val > 0 and self.acq[eng].get(sem, 0) < val:
                    o = _Op(eng, None, [], [], inc=False, label=label)
                    o.waits = [(sem, val)]
                    self.ops.append(o)
                    self._acquire(eng, (sem, val))

    def final_waits(self, eng="sync"):
        for sem, val in self.cnt.items():
            if sem.startswith("D") and val > 0 and self.acq[eng].get(sem, 0) < val:
                o = _Op(eng, None, [], [], inc=False, label="final")
                o.waits = [(sem, val)]
                self.ops.append(o)
                self._acquire(eng, (sem, val))

    maxop = None

    def replay(self, nc, block, sems):
        def emit_stream(name):
            def run(eobj):
                for oi, op in enumerate(self.ops):
                    if self.maxop is not None and oi >= self.maxop:
                        break
                    if op.eng != name:
                        continue
                    for ev in op.waits:
                        eobj.wait_ge(sems[ev[0]], ev[1])
                    if op.emit is None:
                        continue
                    instr = op.emit(eobj)
                    if op.event is not None:
                        step = 16 if op.dma_sem is not None else 1
                        instr.then_inc(sems[op.event[0]], step)
            return run
        block.tensor(emit_stream("tensor"))
        block.vector(emit_stream("vector"))
        block.scalar(emit_stream("scalar"))
        block.gpsimd(emit_stream("gpsimd"))
        block.sync(emit_stream("sync"))

    def used_sems(self):
        s = set()
        for op in self.ops:
            for ev in op.waits:
                s.add(ev[0])
            if op.event is not None:
                s.add(op.event[0])
        return sorted(s)


def build_nc(T=8192, D=2048, dbg=False, maxop=None):
    DC = D // 128
    SEG = T // 4
    NT = SEG // 128
    NCK = NT
    TCH = min(512, SEG)
    NTCH = SEG // TCH
    NSEG = 4
    FS = D // 8
    invs8 = 0.5 / math.sqrt(8.0)
    DKmh = 128.0 ** (-0.5)

    nc = bass.Bass("TRN2", target_bir_lowering=False, debug=False)

    def din(name, shape, dt=F32):
        return nc.dram_tensor(name, shape, dt, kind="ExternalInput").ap()

    xsh = din("xsh", [SEG, D])
    wqkv_s = din("wqkv_s", [D, 512])
    wz_s = din("wz_s", [D, 256])
    wab_s = din("wab_s", [D, 4])
    fact_s = din("fact_s", [8, D, FS])
    fbias = din("fbias", [1, D])
    wout_in = din("wout_in", [2048, D])
    convw_s = din("convw_s", [512, 4])
    dtb_s = din("dtb_s", [1, 2])
    alog_s = din("alog_s", [1, 2])
    normw = din("normw", [1, 128])
    IDbf = din("IDbf", [128, 128], BF16)
    IDf = din("IDf", [128, 128], F32)
    UINC = din("UINC", [128, 128], F32)
    MASKU = din("MASKU", [128, 128], F32)
    MASKL = din("MASKL", [128, 128], F32)
    ONESROW = din("ONESROW", [1, 128], F32)
    ONECOL = din("ONECOL", [128, 1], F32)

    ysh = nc.dram_tensor("ysh", [SEG, D], F32, kind="ExternalOutput").ap()

    fsag_in = nc.dram_tensor("fsag_in", [D, FS], BF16, kind="Internal").ap()
    fsag_out = nc.dram_tensor("fsag_out", [NCORES * D, FS], BF16, kind="Internal", addr_space="Shared").ap()
    H2P = 1
    HSEG = SEG // H2P
    h2ag_ins = [nc.dram_tensor(f"h2ag_in{p}", [HSEG, D], BF16, kind="Internal").ap()
                for p in range(H2P)]
    h2ag_outs = [nc.dram_tensor(f"h2ag_out{p}", [NCORES * HSEG, D], BF16, kind="Internal",
                                addr_space="Shared").ap() for p in range(H2P)]
    ogag_in = nc.dram_tensor("ogag_in", [256, 2 * T], BF16, kind="Internal").ap()
    ogag_out = nc.dram_tensor("ogag_out", [NCORES * 256, 2 * T], BF16, kind="Internal", addr_space="Shared").ap()
    x1sp = nc.dram_tensor("x1sp", [SEG, D], F32, kind="Internal").ap()
    x2sp = nc.dram_tensor("x2sp", [SEG, D], F32, kind="Internal").ap()
    dbgt = nc.dram_tensor("dbgt", [40, 128, 128], F32, kind="Internal").ap() if dbg else None

    from contextlib import ExitStack
    es = ExitStack()

    def sb(name, shape, dt):
        return es.enter_context(nc.sbuf_tensor(name, shape, dt))

    idbf = sb("idbf", [128, 128], BF16)
    idf = sb("idf", [128, 128], F32)
    masku = sb("masku", [128, 128], F32)
    maskl = sb("maskl", [128, 128], F32)
    onesrow = sb("onesrow", [1, 128], F32)
    biasb = sb("biasb", [128, D], F32)
    nwb = sb("nwb", [128, 128], F32)
    dtbb = sb("dtbb", [128, 2], F32)
    nalogb = sb("nalogb", [128, 2], F32)
    srw = sb("srw", [1, 256], F32)
    convw = sb("convw", [128, 16], F32)
    gpk = sb("gpk", [128, 8 * NCK * 4], F32)
    epsc = sb("epsc", [128, 3], F32)
    bgs = sb("bgs", [128, 512], F32)
    gtmp = sb("gtmp", [128, 2 * 64], F32)
    gd2 = [sb(f"gd2_{s}", [128, 2], F32) for s in range(2)]
    grow2 = [sb(f"grow2_{s}", [2, 128], F32) for s in range(2)]
    gcs2 = [sb(f"gcs2_{s}", [2, 128], F32) for s in range(2)]
    ones2 = sb("ones2", [2, 128], F32)
    ones128 = sb("ones128", [128, 128], F32)
    lnsc = sb("lnsc", [128, 8], F32)
    lnsc2 = sb("lnsc2", [128, 8], F32)

    x0t = sb("x0t", [128, D], F32)
    x0t2 = sb("x0t2", [128, D], F32)
    x1t = sb("x1t", [128, D], F32)
    hbf = sb("hbf", [128, D], BF16)

    S_f = [sb(f"Sf{i}", [128, 128], F32) for i in range(4)]
    S_b = [sb(f"Sb{i}", [128, 128], BF16) for i in range(4)]
    E1 = [sb(f"E1_{i}", [128, 128], F32) for i in range(4)]
    E2 = [sb(f"E2_{i}", [128, 128], F32) for i in range(4)]
    Xb = [sb(f"Xb{i}", [128, 128], BF16) for i in range(4)]
    Yb = [sb(f"Yb{i}", [128, 128], BF16) for i in range(4)]
    PYa = [sb(f"PYa{i}", [128, 256], BF16) for i in range(4)]
    PYb = [sb(f"PYb{i}", [128, 256], BF16) for i in range(4)]
    Xs = [sb(f"Xs{i}", [128, 128], BF16) for i in range(4)]
    Vbs = [[sb(f"Vbs{i}_{p}", [128, 128], BF16) for p in range(2)] for i in range(4)]
    Ktok = [[sb(f"Ktok{i}_{p}", [128, 128], BF16) for p in range(2)] for i in range(4)]
    Rb = [sb(f"Rb{i}", [128, 128], BF16) for i in range(4)]
    Ub = [sb(f"Ub{i}", [128, 128], BF16) for i in range(4)]
    Upb = [sb(f"Upb{i}", [128, 128], BF16) for i in range(4)]
    Gb = [[sb(f"Gb{i}_{p}", [128, 128], BF16) for p in range(2)] for i in range(4)]
    Pf = [[sb(f"Pf{i}_{p}", [128, 128], BF16) for p in range(2)] for i in range(4)]
    Otmp = [sb(f"Otmp{i}", [128, 128], F32) for i in range(4)]
    Of = [sb(f"Of{i}", [128, 128], F32) for i in range(4)]
    cols = [sb(f"cols{i}", [128, 32], F32) for i in range(4)]

    wqkvb = sb("wqkvb", [128, DC * 512], BF16)
    wzb = sb("wzb", [128, DC * 256], BF16)
    wabb = sb("wabb", [128, DC * 4], BF16)

    a1w = max((DC * FS) + (DC * D // 2) + 5 * D // 2,
              2 * (DC * TCH // 2) + (8 * SEG // 2) + (4 * SEG // 2) + 4 * ((TCH + 8) // 2) + (4 * SEG // 2) + (16 * 128 // 2),
              (16 * D // 2) + 2 * (16 * 128 // 2),
              (DC * D // 2))
    AR1 = sb("AR1", [128, a1w], F32)
    AR1b = AR1.bitcast(BF16)

    def a1f(off, colsf):
        return AR1[:, off:off + colsf]

    def a1b(off, colsb):
        return AR1b[:, 2 * off:2 * off + colsb]

    facc = lambda: a1f(0, DC * FS)
    fsumA = lambda: a1b(DC * FS, DC * D)
    o_h2 = 0
    o_qkvT = o_h2 + 2 * (DC * TCH // 2)
    o_zsT = o_qkvT + (8 * SEG) // 2
    o_stag = o_zsT + (4 * SEG) // 2
    o_ogsb = o_stag + 4 * ((TCH + 8) // 2)
    o_convd = o_ogsb + (4 * SEG) // 2
    h2tb = lambda i: a1b(o_h2 + i * (DC * TCH // 2), DC * TCH)
    qkvT = lambda: a1b(o_qkvT, 8 * SEG)
    zsT = lambda: a1b(o_zsT, 4 * SEG)
    stagW = lambda cc: a1b(o_stag + cc * ((TCH + 8) // 2), TCH + 8)
    ogsb = lambda: a1b(o_ogsb, 4 * SEG)
    convd = lambda: a1b(o_convd, 16 * 128)
    woutb = lambda: a1b(0, 16 * D)
    ogt2_o = 16 * D // 2
    fsumC = lambda: a1b(0, DC * D)
    o_par = DC * FS + (DC * D) // 2
    htT_p = [a1b(o_par, DC * 128), a1b(o_par + D // 2, DC * 128)]
    hbf_p = [None, a1b(o_par + D, D)]
    x1t_p = [None, None]



    dbgsb = sb("dbgsb", [128, 40 * 128], F32) if dbg else None
    bk = [es.enter_context(nc.psum_tensor(f"bk{i}", [128, 512], F32)) for i in range(8)]
    bkb = [b.bitcast(BF16) for b in bk]
    hbf_p[0] = hbf[:]
    x1t_p[0] = x1t[:]
    x1t_p[1] = a1f(o_par + 3 * D // 2, D)
    lnsc_p = [lnsc, lnsc2]

    P = Prog()
    rg = [[i for i in range(NCORES)]]

    P.dma("sync", lambda e: e.dma_start(out=idbf[:], in_=IDbf), outs=["idbf"], role="c0")
    P.dma("sync", lambda e: e.dma_start(out=idf[:], in_=IDf), outs=["idf"], role="c0")
    P.dma("sync", lambda e: e.dma_start(out=masku[:], in_=MASKU), outs=["masku"], role="c1")
    P.dma("sync", lambda e: e.dma_start(out=maskl[:], in_=MASKL), outs=["maskl"], role="c1")
    P.dma("sync", lambda e: e.dma_start(out=onesrow[:], in_=ONESROW), outs=["onesrow"], role="c1")
    brow = x1t[0:1, 0:D]
    P.dma("sync", lambda e: e.dma_start(out=brow, in_=fbias), outs=["x1t"], role="c2")
    P.dma("sync", lambda e: e.dma_start(out=srw[0:1, 0:2], in_=dtb_s), outs=["srw"], role="c2")
    P.dma("sync", lambda e: e.dma_start(out=srw[0:1, 4:6], in_=alog_s), outs=["srw"], role="c2")
    P.dma("sync", lambda e: e.dma_start(out=srw[0:1, 8:136], in_=normw), outs=["srw"], role="c2")
    for q in range(4):
        P.dma("gpsimd", lambda e, q=q: e.dma_start(out=convw[:, q * 4:(q + 1) * 4],
                                                   in_=convw_s[q * 128:(q + 1) * 128, :]),
              outs=["convw"], role="g0")

    if dbg:
        P.op("vector", lambda e: e.memset(dbgsb[:], 0.0), ins=[], outs=["dbgsb"], label="dbg0")
    P.op("vector", lambda e: e.memset(epsc[:, 0:1], 1e-5), ins=[], outs=["epsc"], label="eps5")
    P.op("vector", lambda e: e.memset(epsc[:, 1:2], 1e-6), ins=[], outs=["epsc"], label="eps6")
    P.op("vector", lambda e: e.memset(epsc[:, 2:3], 1.0), ins=[], outs=["epsc"], label="one")
    P.op("vector", lambda e: e.memset(ones2[:], 1.0), ins=[], outs=["ones2"], label="ones2")
    P.op("vector", lambda e: e.memset(ones128[:], 1.0), ins=[], outs=["ones128"], label="ones128")
    P.op("scalar", lambda e: e.activation(srw[0:1, 6:8], srw[0:1, 4:6], AF.Exp),
         ins=["srw"], outs=["srw"], label="expalog")
    P.op("vector", lambda e: e.tensor_scalar(srw[0:1, 6:8], srw[0:1, 6:8], -1.0, None, ALU.mult),
         ins=["srw"], outs=["srw"], label="negalog")

    for j in range((D + 511) // 512):
        w = min(512, D - j * 512)
        P.op("tensor", lambda e, j=j, w=w: e.matmul(bk[0][:, 0:w], onesrow[:],
                                                    x1t[0:1, j * 512:j * 512 + w], start=True, stop=True),
             ins=["onesrow", "x1t"], outs=["bk0"], label="biasbc")
        P.op("vector", lambda e, j=j, w=w: e.tensor_scalar(biasb[:, j * 512:j * 512 + w],
                                                           bk[0][:, 0:w], 0.5, None, ALU.mult),
             ins=["bk0"], outs=["biasb"], label="biasev")
    P.op("tensor", lambda e: e.matmul(bk[0][:, 0:128], onesrow[:], srw[0:1, 8:136], start=True, stop=True),
         ins=["onesrow", "srw"], outs=["bk0"], label="nwbc")
    P.op("vector", lambda e: e.tensor_copy(nwb[:], bk[0][:, 0:128]), ins=["bk0"], outs=["nwb"])
    P.op("tensor", lambda e: e.matmul(bk[0][:, 0:2], onesrow[:], srw[0:1, 0:2], start=True, stop=True),
         ins=["onesrow", "srw"], outs=["bk0"], label="dtbbc")
    P.op("vector", lambda e: e.tensor_copy(dtbb[:], bk[0][:, 0:2]), ins=["bk0"], outs=["dtbb"])
    P.op("tensor", lambda e: e.matmul(bk[0][:, 0:2], onesrow[:], srw[0:1, 6:8], start=True, stop=True),
         ins=["onesrow", "srw"], outs=["bk0"], label="nalogbc")
    P.op("vector", lambda e: e.tensor_copy(nalogb[:], bk[0][:, 0:2]), ins=["bk0"], outs=["nalogb"])

    P.dma("gpsimd", lambda e: e.dma_start(
        out=wqkvb[:].rearrange("p (dc c) -> p dc c", c=512),
        in_=wqkv_s.rearrange("(dc p) c -> p dc c", p=128)), outs=["wqkvb"], role="g1")
    P.dma("gpsimd", lambda e: e.dma_start(
        out=wzb[:].rearrange("p (dc c) -> p dc c", c=256),
        in_=wz_s.rearrange("(dc p) c -> p dc c", p=128)), outs=["wzb"], role="g2")
    P.dma("gpsimd", lambda e: e.dma_start(
        out=wabb[:].rearrange("p (dc c) -> p dc c", c=4),
        in_=wab_s.rearrange("(dc p) c -> p dc c", p=128)), outs=["wabb"], role="g2")

    qdc = max(1, DC // 4)
    NQ = DC // qdc
    qw = qdc * FS
    for r in range(8):
        kw = {} if r == 0 else dict(accum_op=ALU.add)
        for q in range(NQ):
            P.dma("gpsimd", lambda e, r=r, q=q, kw=kw: e.dma_start(
                out=facc().rearrange("p (dc c) -> p dc c", c=FS)[:, q * qdc:(q + 1) * qdc, :],
                in_=fact_s[r][q * qdc * 128:(q + 1) * qdc * 128, :].rearrange("(dc p) c -> p dc c", p=128),
                **kw),
                ins=([f"facc{q}"] if r else []), outs=[f"facc{q}"], role=f"fa{q}", label=f"fact{q}_{r}")
    nsc = (DC * FS + 511) // 512
    for j in range(nsc):
        w = min(512, DC * FS - j * 512)
        ks = [f"facc{q}" for q in range((j * 512) // qw, min(NQ - 1, (j * 512 + w - 1) // qw) + 1)]
        P.op("vector", lambda e, j=j, w=w: e.tensor_scalar(
            facc()[:, j * 512:j * 512 + w], facc()[:, j * 512:j * 512 + w], invs8, None, ALU.mult),
            ins=ks, outs=ks, label="fscale")
    P.dma("gpsimd", lambda e: e.dma_start(
        out=fsag_in.rearrange("(dc p) c -> p dc c", p=128),
        in_=facc().rearrange("p (dc c) -> p dc c", c=FS)),
        ins=[f"facc{q}" for q in range(NQ)], outs=["fsag_in"], role="g3")
    P.op("gpsimd", lambda e: e.collective_compute("AllGather", ALU.bypass, rg, [fsag_in], [fsag_out]),
         ins=["fsag_in"], outs=["fsag_out"], label="AGfsum")
    for r in range(NCORES):
        P.dma("gpsimd", lambda e, r=r: e.dma_start(
            out=fsumA().rearrange("p (dc rr c) -> p dc rr c", rr=NCORES, c=FS)[:, :, r, :],
            in_=fsag_out[r * D:(r + 1) * D, :].rearrange("(dc p) c -> p dc c", p=128)),
            ins=["fsag_out"], outs=["fsumb"], role=f"g{4 + r % 2}", label=f"fsld{r}")

    def ln_tile_gen(src_key, src, dst, dst_key, tag, par):
        lsc = lnsc_p[par]
        lkey = f"lnsc{par}"
        P.op("vector", lambda e: e.tensor_reduce(lsc[:, 0:1], src[:], AX.X, ALU.add),
             ins=[src_key], outs=[lkey], label=f"lnsum{tag}")
        yield
        P.op("scalar", lambda e: e.activation(dst[:], src[:], AF.Square, accum_out=lsc[:, 1:2]),
             ins=[src_key], outs=[dst_key, lkey], label=f"lnsq{tag}")
        yield
        P.op("vector", lambda e: e.tensor_scalar(lsc[:, 2:3], lsc[:, 0:1], 1.0 / D, None, ALU.mult),
             ins=[lkey], outs=[lkey], label="lnm")
        P.op("vector", lambda e: e.tensor_tensor(lsc[:, 3:4], lsc[:, 2:3], lsc[:, 2:3], ALU.mult),
             ins=[lkey], outs=[lkey], label="lnm2")
        P.op("vector", lambda e: e.tensor_scalar(lsc[:, 4:5], lsc[:, 1:2], 1.0 / D, lsc[:, 3:4],
                                                 ALU.mult, ALU.subtract),
             ins=[lkey], outs=[lkey], label="lnvar")
        yield
        P.op("scalar", lambda e: e.activation(lsc[:, 5:6], lsc[:, 4:5], AF.Ln, bias=epsc[:, 0:1]),
             ins=[lkey, "epsc"], outs=[lkey], label="lnrstd")
        P.op("scalar", lambda e: e.activation(lsc[:, 5:6], lsc[:, 5:6], AF.Exp, scale=-0.5),
             ins=[lkey], outs=[lkey], label="lnrstd2")
        yield
        P.op("vector", lambda e: e.tensor_scalar(lsc[:, 6:7], lsc[:, 2:3], -1.0, lsc[:, 5:6],
                                                 ALU.mult, ALU.mult),
             ins=[lkey], outs=[lkey], label="lnmr")
        yield
        P.op("vector", lambda e: e.tensor_scalar(dst[:], src[:], lsc[:, 5:6], lsc[:, 6:7],
                                                 ALU.mult, ALU.add),
             ins=[src_key, lkey], outs=[dst_key], label=f"lnap{tag}")
        yield

    def transpose_gen(src_ap, src_key, dst, dst_key, n, bank, par):
        for g in range(0, n, 4):
            cnt = min(4, n - g)
            for i2 in range(cnt):
                P.op("tensor", lambda e, g=g, i2=i2: e.transpose(
                    bkb[bank][:, i2 * 128:(i2 + 1) * 128], src_ap[:, (g + i2) * 128:(g + i2 + 1) * 128],
                    idbf[:]),
                    ins=[src_key, "idbf"], outs=[f"bk{bank}"], inc=(i2 == cnt - 1), label="tp")
            eng = "scalar" if (g // 4) % 2 == 0 else "vector"
            if eng == "scalar":
                P.op(eng, lambda e, g=g, cnt=cnt: e.copy(
                    dst[:, g * 128:(g + cnt) * 128], bkb[bank][:, 0:cnt * 128]),
                    ins=[f"bk{bank}"], outs=[dst_key], label="tpev")
            else:
                P.op(eng, lambda e, g=g, cnt=cnt: e.tensor_copy(
                    dst[:, g * 128:(g + cnt) * 128], bkb[bank][:, 0:cnt * 128]),
                    ins=[f"bk{bank}"], outs=[dst_key], label="tpev")
            yield

    ne = (D + 511) // 512

    def ffnA_gen(t):
        par = t % 2
        xb, xkey = (x0t, "x0A") if par == 0 else (x0t2, "x0B")
        hb, hkey2 = hbf_p[par], f"hbfp{par}"
        ht, htkey = htT_p[par], f"htTp{par}"
        xo, xokey = x1t_p[par], f"x1tp{par}"
        base = 4 * par
        P.dma("sync", lambda e, t=t, xb=xb: e.dma_start(out=xb[:], in_=xsh[t * 128:(t + 1) * 128, :]),
              outs=[xkey], role=f"x{par}", label=f"x0ld{t}")
        yield
        yield from ln_tile_gen(xkey, xb, hb, hkey2, f"A{t}", par)
        yield from transpose_gen(hb, hkey2, ht, htkey, DC, base + 3, par)
        for eb in range(ne):
            w = min(512, D - eb * 512)
            for dcb in range(DC):
                P.op("tensor", lambda e, eb=eb, w=w, dcb=dcb, ht=ht: e.matmul(
                    bk[base + eb][:, 0:w], ht[:, dcb * 128:(dcb + 1) * 128],
                    fsumA()[:, dcb * D + eb * 512: dcb * D + eb * 512 + w],
                    start=(dcb == 0), stop=(dcb == DC - 1)),
                    ins=[htkey, "fsumb"], outs=[f"bk{base + eb}"], inc=(dcb == DC - 1), label="ffn1mm")
            yield
        beng = "gpsimd" if t < NT // H2P else "vector"
        P.op(beng, lambda e, xb=xb: e.tensor_add(xb[:], xb[:], biasb[:]),
             ins=[xkey, "biasb"], outs=[xkey], label="ffn1bias")
        yield
        for eb in range(ne):
            w = min(512, D - eb * 512)
            P.op("vector", lambda e, eb=eb, w=w, xb=xb, xo=xo: e.tensor_add(
                xo[:, eb * 512:eb * 512 + w], bk[base + eb][:, 0:w], xb[:, eb * 512:eb * 512 + w]),
                ins=[f"bk{base + eb}", xkey], outs=[xokey], label="ffn1res")
            yield
        P.dma("sync", lambda e, t=t, xo=xo: e.dma_start(out=x1sp[t * 128:(t + 1) * 128, :], in_=xo[:]),
              ins=[xokey], outs=[f"x1sp{t}"], role="xs", label=f"x1sp{t}")
        yield
        yield from ln_tile_gen(xokey, xo, hb, hkey2, f"A2{t}", par)
        hp = t // (NT // H2P)
        hoff = (t % (NT // H2P)) * 128
        P.dma("sync", lambda e, hb=hb, hp=hp, hoff=hoff: e.dma_start(
            out=h2ag_ins[hp][hoff:hoff + 128, :], in_=hb[:]),
            ins=[hkey2], outs=[f"h2w{t}"], role="hw", label=f"h2w{t}")
        yield

    for hp_ in range(H2P):
        lo, hi = hp_ * (NT // H2P), (hp_ + 1) * (NT // H2P)
        for t0_ in range(lo, hi, 2):
            drive2 = [ffnA_gen(t0_), ffnA_gen(t0_ + 1)] if t0_ + 1 < hi else [ffnA_gen(t0_)]
            while drive2:
                keep = []
                for g in drive2:
                    try:
                        next(g)
                        keep.append(g)
                    except StopIteration:
                        pass
                drive2 = keep
        P.op("gpsimd", lambda e, hp_=hp_: e.collective_compute(
            "AllGather", ALU.bypass, rg, [h2ag_ins[hp_]], [h2ag_outs[hp_]]),
            ins=[f"h2w{t}" for t in range(lo, hi)], outs=[f"h2ag{hp_}"], label=f"AGh2{hp_}")

    P.hard_barrier("A2B")

    for cc in range(4):
        for j in range(4):
            P.op("vector", lambda e, cc=cc, j=j: e.tensor_scalar(
                convd()[:, (cc * 4 + j) * 128:(cc * 4 + j + 1) * 128], idbf[:],
                convw[:, cc * 4 + j:cc * 4 + j + 1], None, ALU.mult),
                ins=["idbf", "convw"], outs=["convd"], label="convd")
    for cc in range(4):
        for s01 in range(2):
            P.op("vector", lambda e, cc=cc: e.memset(stagW(cc)[:, 0:4], 0.0),
                 ins=[], outs=[f"stail{cc}_{s01}"], label="stail0") if s01 == 0 else None
    tails = sb("tails", [128, 4 * 2 * 4], BF16)
    for cc in range(4):
        for s01 in range(2):
            P.op("vector", lambda e, cc=cc, s01=s01: e.memset(
                tails[:, (cc * 2 + s01) * 4:(cc * 2 + s01) * 4 + 4], 0.0),
                ins=[], outs=[f"tail{cc}_{s01}"], label="tail0")
    for i in range(4):
        P.op("vector", lambda e, i=i: e.memset(cols[i][:], 0.0), ins=[], outs=[f"cols{i}_0", f"cols{i}_1"], label="cols0")
        P.op("vector", lambda e, i=i: e.memset(S_f[i][:], 0.0), ins=[], outs=[f"Sf{i}"], label="S0")
        P.op("vector", lambda e, i=i: e.memset(S_b[i][:], 0.0), ins=[], outs=[f"Sb{i}"], label="S0b")

    _dbgn = [0]

    def dump(slot, ap_fn, key, w=128, f32=False):
        if not dbg:
            return
        P.op("vector", lambda e: e.tensor_copy(dbgsb[:, slot * 128: slot * 128 + w], ap_fn()),
             ins=[key], outs=["dbgsb"], label=f"dump{slot}")

    def proj_block(pair, s01):
        rb = pair + 4 * s01
        qkey = f"qkvT{s01}"
        zkey = f"zsT{s01}"
        zT = None
        for tc in range(NTCH):
            hb = h2tb((tc + pair * NTCH) % 2)
            hkey = f"h2tb{(tc + pair * NTCH) % 2}"
            t0 = tc * TCH
            hp = t0 // HSEG
            hrow = rb * HSEG + (t0 % HSEG)
            for dcb in range(DC):
                P.dma_pool("sync", lambda e, hp=hp, hrow=hrow, dcb=dcb, hb=hb: e.dma_start(
                    out=hb[:, dcb * TCH:(dcb + 1) * TCH],
                    in_=h2ag_outs[hp][hrow: hrow + TCH, dcb * 128:(dcb + 1) * 128],
                    transpose=True),
                    ins=[f"h2ag{hp}"], outs=[hkey], pool="ht", npool=4, label=f"h2r{rb}_{tc}_{dcb}")
            for cc in range(4):
                ckk = f"sw{cc}"
                pb = cc % 2
                P.op("vector", lambda e, cc=cc, s01=s01: e.tensor_copy(
                    stagW(cc)[:, 0:4], tails[:, (cc * 2 + s01) * 4:(cc * 2 + s01) * 4 + 4]),
                    ins=[f"tail{cc}_{s01}"], outs=[ckk], label="tin")
                for dcb in range(DC):
                    P.op("tensor", lambda e, cc=cc, dcb=dcb, hb=hb, pb=pb: e.matmul(
                        bk[pb][:, 0:TCH],
                        wqkvb[:, dcb * 512 + cc * 128: dcb * 512 + (cc + 1) * 128],
                        hb[:, dcb * TCH:(dcb + 1) * TCH],
                        start=(dcb == 0), stop=(dcb == DC - 1)),
                        ins=["wqkvb", hkey], outs=[f"bk{pb}"], inc=(dcb == DC - 1), label="qkvmm")
                P.op("scalar", lambda e, cc=cc, pb=pb: e.activation(stagW(cc)[:, 4:4 + TCH], bk[pb][:, 0:TCH], AF.Copy),
                     ins=[f"bk{pb}"], outs=[ckk], label="mixev")
                P.op("vector", lambda e, cc=cc, s01=s01: e.tensor_copy(
                    tails[:, (cc * 2 + s01) * 4:(cc * 2 + s01) * 4 + 4], stagW(cc)[:, TCH:TCH + 4]),
                    ins=[ckk], outs=[f"tail{cc}_{s01}"], label="tout")
                for j in range(4):
                    P.op("tensor", lambda e, cc=cc, j=j, pb=pb: e.matmul(
                        bk[pb][:, 0:TCH], convd()[:, (cc * 4 + j) * 128:(cc * 4 + j + 1) * 128],
                        stagW(cc)[:, 1 + j: 1 + j + TCH],
                        start=(j == 0), stop=(j == 3)),
                        ins=["convd", ckk], outs=[f"bk{pb}"], inc=(j == 3), label="convmm")
                P.op("scalar", lambda e, cc=cc, t0=t0, pb=pb, s01=s01: e.activation(
                    qkvT()[:, (cc * 2 + s01) * SEG + t0: (cc * 2 + s01) * SEG + t0 + TCH],
                    bk[pb][:, 0:TCH], AF.Silu),
                    ins=[f"bk{pb}"], outs=[qkey], label="qsilu")
            for cc in range(2):
                for dcb in range(DC):
                    P.op("tensor", lambda e, cc=cc, dcb=dcb, hb=hb: e.matmul(
                        bk[0][:, 0:TCH],
                        wzb[:, dcb * 256 + cc * 128: dcb * 256 + (cc + 1) * 128],
                        hb[:, dcb * TCH:(dcb + 1) * TCH],
                        start=(dcb == 0), stop=(dcb == DC - 1)),
                        ins=["wzb", hkey], outs=["bk0"], inc=(dcb == DC - 1), label="zmm")
                P.op("scalar", lambda e, cc=cc, t0=t0, s01=s01: e.activation(
                    zsT()[:, (cc * 2 + s01) * SEG + t0: (cc * 2 + s01) * SEG + t0 + TCH],
                    bk[0][:, 0:TCH], AF.Silu),
                    ins=["bk0"], outs=[zkey], label="zsilu")
            for dcb in range(DC):
                P.op("tensor", lambda e, dcb=dcb, hb=hb: e.matmul(
                    bk[1][0:4, 0:TCH], wabb[:, dcb * 4:(dcb + 1) * 4],
                    hb[:, dcb * TCH:(dcb + 1) * TCH],
                    start=(dcb == 0), stop=(dcb == DC - 1)),
                    ins=["wabb", hkey], outs=["bk1"], inc=(dcb == DC - 1), label="bgmm")
            P.op("vector", lambda e: e.tensor_copy(bgs[0:4, 0:TCH], bk[1][0:4, 0:TCH]),
                 ins=["bk1"], outs=["bgs"], label="bgst")
            ntt = TCH // 128
            for k in range(ntt):
                gtile = rb * NCK + (t0 // 128) + k
                P.op("tensor", lambda e, k=k, gtile=gtile: e.transpose(
                    bk[1][:, 128 + k * 4: 128 + (k + 1) * 4], bgs[0:4, k * 128:(k + 1) * 128], idf[0:4, 0:4]),
                    ins=["bgs", "idf"], outs=["bk1"], inc=(k == ntt - 1), label="bgT")
            P.op("vector", lambda e, t0=t0, rb=rb, ntt=ntt: e.tensor_copy(
                gpk[:, (rb * NCK + t0 // 128) * 4: (rb * NCK + t0 // 128 + ntt) * 4],
                bk[1][:, 128:128 + ntt * 4]),
                ins=["bk1"], outs=["gpk"], label="bgTev")

    def bg_acts(pair):
        gv = gpk[:].rearrange("p (k c) -> p k c", c=4)
        for s01 in range(2):
            rb = pair + 4 * s01
            k0 = rb * NCK
            P.op("scalar", lambda e, k0=k0: e.activation(
                gv[:, k0:k0 + NCK, 0:2], gv[:, k0:k0 + NCK, 0:2], AF.Sigmoid),
                ins=["gpk"], outs=["gpk"], label="bsig")
            for h in range(2):
                P.op("vector", lambda e, k0=k0, h=h: e.tensor_scalar(
                    gv[:, k0:k0 + NCK, 2 + h:3 + h], gv[:, k0:k0 + NCK, 2 + h:3 + h],
                    dtbb[:, h:h + 1], None, ALU.add),
                    ins=["gpk", "dtbb"], outs=["gpk"], label="gdtb")
                P.op("scalar", lambda e, k0=k0, h=h: e.activation(
                    gtmp[:, 0:NCK], gv[:, k0:k0 + NCK, 2 + h:3 + h], AF.Abs),
                    ins=["gpk"], outs=["gtmp"], label="gabs")
                P.op("scalar", lambda e: e.activation(gtmp[:, 0:NCK], gtmp[:, 0:NCK], AF.Exp, scale=-1.0),
                     ins=["gtmp"], outs=["gtmp"], label="gexp")
                P.op("scalar", lambda e: e.activation(gtmp[:, 0:NCK], gtmp[:, 0:NCK], AF.Ln, bias=epsc[:, 2:3]),
                     ins=["gtmp", "epsc"], outs=["gtmp"], label="gln")
                P.op("vector", lambda e, k0=k0, h=h: e.tensor_relu(
                    gtmp[:, 64:64 + NCK], gv[:, k0:k0 + NCK, 2 + h:3 + h]),
                    ins=["gpk"], outs=["gtmp"], label="grelu")
                P.op("vector", lambda e: e.tensor_add(gtmp[:, 0:NCK], gtmp[:, 0:NCK], gtmp[:, 64:64 + NCK]),
                     ins=["gtmp"], outs=["gtmp"], label="gadd")
                P.op("vector", lambda e, k0=k0, h=h: e.tensor_scalar(
                    gv[:, k0:k0 + NCK, 2 + h:3 + h], gtmp[:, 0:NCK],
                    nalogb[:, h:h + 1], None, ALU.mult),
                    ins=["gtmp", "nalogb"], outs=["gpk"], label="gsc")

    GC, NGC, BB, BETA, SS, SQ, S2, BS, MBBS2, MBS2, BC, KKD, QQD, M2, OSC, BCB = range(16)

    def head_chunk(pair, s01, ck, vh, par):
        rb = pair + 4 * s01
        gtile = rb * NCK + ck
        i = s01 * 2 + vh
        HB = bk[s01 * 2 + vh]
        HBk = f"bk{s01 * 2 + vh}"
        GB = bk[s01 * 2]
        GBk = f"bk{s01 * 2}"
        qkey = f"qkvT{s01}"
        Ktp = lambda: qkvT()[:, (1 * 2 + s01) * SEG + ck * 128: (1 * 2 + s01) * SEG + (ck + 1) * 128]
        Qtp = lambda: qkvT()[:, (0 * 2 + s01) * SEG + ck * 128: (0 * 2 + s01) * SEG + (ck + 1) * 128]
        Vtp = lambda: qkvT()[:, ((2 + vh) * 2 + s01) * SEG + ck * 128: ((2 + vh) * 2 + s01) * SEG + (ck + 1) * 128]
        col = cols[i][:, 16 * par:16 * par + 16]
        ckey = f"cols{i}_{par}"
        gv = gpk[:].rearrange("p (k c) -> p k c", c=4)
        ish = s01 * 2
        if vh == 0:
            P.op("tensor", lambda e: e.matmul(GB[:, 0:128], Ktp(), Ktp(), start=True, stop=True),
                 ins=[qkey], outs=[GBk], inc=False, label="kk")
            P.op("tensor", lambda e: e.matmul(GB[:, 128:256], Ktp(), Qtp(), start=True, stop=True),
                 ins=[qkey], outs=[GBk], inc=False, label="kq")
            P.op("tensor", lambda e: e.matmul(GB[:, 256:384], Qtp(), Qtp(), start=True, stop=True),
                 ins=[qkey], outs=[GBk], label="qq")
            yield
            P.op("vector", lambda e: e.tensor_mul(E1[ish][:], GB[:, 0:128], idf[:]),
                 ins=[GBk, "idf"], outs=[f"E1_{ish}"], label="kkdm")
            yield
            P.op("vector", lambda e: e.tensor_mul(E2[ish][:], GB[:, 256:384], idf[:]),
                 ins=[GBk, "idf"], outs=[f"E2_{ish}"], label="qqdm")
            yield
            yield
            P.op("vector", lambda e: e.tensor_reduce(cols[ish][:, 16 * par + KKD:16 * par + KKD + 1],
                                                     E1[ish][:], AX.X, ALU.add),
                 ins=[f"E1_{ish}"], outs=[f"cols{ish}_{par}"], label="kkdred")
            yield
            P.op("vector", lambda e: e.tensor_reduce(cols[ish][:, 16 * par + QQD:16 * par + QQD + 1],
                                                     E2[ish][:], AX.X, ALU.add),
                 ins=[f"E2_{ish}"], outs=[f"cols{ish}_{par}"], label="qqdred")
            yield
            P.op("scalar", lambda e: e.activation(cols[ish][:, 16 * par + KKD:16 * par + QQD + 1],
                                                  cols[ish][:, 16 * par + KKD:16 * par + QQD + 1],
                                                  AF.Ln, bias=epsc[:, 1:2]),
                 ins=[f"cols{ish}_{par}", "epsc"], outs=[f"cols{ish}_{par}"], label="sln")
            yield
            for ii in (ish, ish + 1):
                P.op("scalar", lambda e, ii=ii: e.activation(
                    cols[ii][:, 16 * par + SS:16 * par + SQ + 1],
                    cols[ish][:, 16 * par + KKD:16 * par + QQD + 1], AF.Exp, scale=-0.5),
                    ins=[f"cols{ish}_{par}"], outs=[f"cols{ii}_{par}"], label="s")
                yield
            yield
            P.op("vector", lambda e: e.tensor_copy(gd2[s01][:], gv[:, gtile, 2:4]),
                 ins=["gpk"], outs=[f"gd2{s01}"], label="gdense")
            yield
            P.op("tensor", lambda e: e.transpose(GB[0:2, 384:512], gd2[s01][:], idf[:]),
                 ins=[f"gd2{s01}", "idf"], outs=[GBk], label="gT")
            yield
            P.op("vector", lambda e: e.tensor_copy(grow2[s01][:], GB[0:2, 384:512]),
                 ins=[GBk], outs=[f"grow2{s01}"], label="growev")
            yield
            P.op("vector", lambda e: e.tensor_tensor_scan(gcs2[s01][:], grow2[s01][:], ones2[:], 0.0,
                                                          ALU.add, ALU.mult),
                 ins=[f"grow2{s01}", "ones2"], outs=[f"gcs2{s01}"], label="gscan")
            yield
            P.op("tensor", lambda e: e.transpose(GB[:, 384:386], gcs2[s01][:], idf[0:2, 0:2]),
                 ins=[f"gcs2{s01}", "idf"], outs=[GBk], label="gcT")
            yield
            for ii in (ish, ish + 1):
                P.op("vector", lambda e, ii=ii: e.tensor_copy(
                    cols[ii][:, 16 * par + GC:16 * par + GC + 1],
                    GB[:, 384 + (ii - ish):385 + (ii - ish)]),
                    ins=[GBk], outs=[f"cols{ii}_{par}"], label="gcev")
                yield
            yield "shared_done"
        P.op("vector", lambda e: e.tensor_scalar(col[:, NGC:NGC + 1], col[:, GC:GC + 1], -1.0, None, ALU.mult),
             ins=[ckey], outs=[ckey], label="ngc")
        yield
        P.op("scalar", lambda e: e.activation(col[:, BB:BB + 1], col[:, GC:GC + 1], AF.Exp),
             ins=[ckey], outs=[ckey], label="bexp")
        yield
        P.op("vector", lambda e: e.tensor_copy(col[:, BETA:BETA + 1], gv[:, gtile, vh:vh + 1]),
             ins=["gpk"], outs=[ckey], label="beta")
        yield
        P.op("vector", lambda e: e.tensor_tensor(col[:, S2:S2 + 1], col[:, SS:SS + 1], col[:, SS:SS + 1], ALU.mult),
             ins=[ckey], outs=[ckey], label="s2")
        yield
        P.op("vector", lambda e: e.tensor_tensor(col[:, BS:BS + 1], col[:, BETA:BETA + 1], col[:, SS:SS + 1], ALU.mult),
             ins=[ckey], outs=[ckey], label="bs")
        yield
        P.op("vector", lambda e: e.tensor_scalar(col[:, MBS2:MBS2 + 1], col[:, S2:S2 + 1], -1.0,
                                                 col[:, BETA:BETA + 1], ALU.mult, ALU.mult),
             ins=[ckey], outs=[ckey], label="mbs2")
        yield
        P.op("vector", lambda e: e.tensor_tensor(col[:, MBBS2:MBBS2 + 1], col[:, MBS2:MBS2 + 1],
                                                 col[:, BB:BB + 1], ALU.mult),
             ins=[ckey], outs=[ckey], label="mbbs2")
        yield
        yield
        P.op("vector", lambda e: e.tensor_scalar(E1[i][:], ones128[:], col[:, GC:GC + 1], None, ALU.mult),
             ins=["ones128", ckey], outs=[f"E1_{i}"], label="colb")
        yield
        P.op("tensor", lambda e: e.transpose(HB[:, 256:384], E1[i][:], idf[:]),
             ins=[f"E1_{i}", "idf"], outs=[HBk], label="gcbT")
        yield
        P.op("vector", lambda e: e.tensor_add(E1[i][:], HB[:, 256:384], masku[:]),
             ins=[HBk, "masku"], outs=[f"E1_{i}"], label="e1a")
        yield
        P.op("vector", lambda e: e.tensor_add(E2[i][:], HB[:, 256:384], maskl[:]),
             ins=[HBk, "maskl"], outs=[f"E2_{i}"], label="e2a")
        yield
        P.op("scalar", lambda e: e.activation(E1[i][:], E1[i][:], AF.Exp, bias=col[:, GC:GC + 1], scale=-1.0),
             ins=[f"E1_{i}", ckey], outs=[f"E1_{i}"], label="E1")
        yield
        P.op("scalar", lambda e: e.activation(E2[i][:], E2[i][:], AF.Exp, bias=col[:, NGC:NGC + 1], scale=1.0),
             ins=[f"E2_{i}", ckey], outs=[f"E2_{i}"], label="E2")
        yield
        yield
        P.op("vector", lambda e: e.tensor_copy(col[:, BCB:BCB + 1], E2[i][:, 127:128]),
             ins=[f"E2_{i}"], outs=[ckey], label="bCb")
        yield
        P.op("vector", lambda e: e.tensor_tensor(col[:, BC:BC + 1], col[:, BCB:BCB + 1], col[:, BB:BB + 1], ALU.mult),
             ins=[ckey], outs=[ckey], label="bC")
        yield
        P.op("vector", lambda e: e.scalar_tensor_tensor(Xb[i][:], GB[:, 0:128], col[:, MBS2:MBS2 + 1],
                                                        E1[i][:], ALU.mult, ALU.mult),
             ins=[GBk, ckey, f"E1_{i}"], outs=[f"Xb{i}"], label="X")
        yield
        P.op("vector", lambda e, par=par: e.tensor_mul(Gb[i][par][:], GB[:, 128:256], E2[i][:]),
             ins=[GBk, f"E2_{i}"], outs=[f"Gb{i}_{par}"], label="G")
        yield
        yield
        P.op("tensor", lambda e: e.transpose(bkb[s01 * 2 + vh][:, 768:896], Xb[i][:], idbf[:]),
             ins=[f"Xb{i}", "idbf"], outs=[HBk], label="XT")
        yield
        P.op("scalar", lambda e: e.activation(Yb[i][:], bkb[s01 * 2 + vh][:, 768:896], AF.Copy),
             ins=[HBk], outs=[f"Yb{i}"], label="YTev")
        yield
        P.op("tensor", lambda e: e.transpose(bkb[s01 * 2 + vh][:, 768:896], Vtp(), idbf[:]),
             ins=[qkey, "idbf"], outs=[HBk], label="VT")
        yield
        P.op("vector", lambda e, par=par: e.tensor_scalar(Vbs[i][par][:], bkb[s01 * 2 + vh][:, 768:896],
                                                          col[:, BS:BS + 1], None, ALU.mult),
             ins=[HBk, ckey], outs=[f"Vbs{i}_{par}"], label="Vbs")
        yield
        P.op("tensor", lambda e: e.transpose(bkb[s01 * 2 + vh][:, 768:896], Ktp(), idbf[:]),
             ins=[qkey, "idbf"], outs=[HBk], label="KT")
        yield
        P.op("scalar", lambda e, par=par: e.activation(Ktok[i][par][:], bkb[s01 * 2 + vh][:, 768:896], AF.Copy),
             ins=[HBk], outs=[f"Ktok{i}_{par}"], label="KTev")
        yield
        yield
        P.op("vector", lambda e: e.tensor_add(PYa[i][:, 0:128], idbf[:], Yb[i][:]),
             ins=["idbf", f"Yb{i}"], outs=[f"PYa{i}"], label="P2")
        yield
        P.op("tensor", lambda e: e.matmul(HB[:, 384:512], Xb[i][:], Yb[i][:], start=True, stop=True),
             ins=[f"Xb{i}", f"Yb{i}"], outs=[HBk], label="Y2mm")
        yield
        P.op("scalar", lambda e: e.activation(PYa[i][:, 128:256], HB[:, 384:512], AF.Copy),
             ins=[HBk], outs=[f"PYa{i}y"], label="Y2ev")
        yield
        P.op("tensor", lambda e: e.matmul(HB[:, 384:512], Yb[i][:], Xb[i][:], start=True, stop=True),
             ins=[f"Yb{i}", f"Xb{i}"], outs=[HBk], label="X2mm")
        yield
        P.op("scalar", lambda e: e.activation(Xs[i][:], HB[:, 384:512], AF.Copy),
             ins=[HBk], outs=[f"Xs{i}"], label="X2ev")
        yield
        yield
        xcur, xck = Xs[i], f"Xs{i}"
        xoth, xok = Xb[i], f"Xb{i}"
        pycur, pyck = PYa[i], f"PYa{i}"
        pyoth, pyok = PYb[i], f"PYb{i}"
        for stp in range(5):
            m = 2 << stp
            P.op("tensor", lambda e, xcur=xcur, pycur=pycur, xck=xck, pyck=pyck: e.matmul(
                HB[:, 0:256], xcur[:], pycur[:], start=True, stop=False),
                ins=[xck, pyck, pyck + "y"], outs=[HBk], inc=False, label=f"dbl{m}")
            P.op("tensor", lambda e, pycur=pycur, pyck=pyck: e.matmul(
                HB[:, 0:128], idbf[:], pycur[:, 0:128], start=False, stop=True),
                ins=["idbf", pyck], outs=[HBk], label=f"dblP{m}")
            yield
            eng = "vector" if stp % 2 == 0 else "scalar"
            if eng == "vector":
                P.op("vector", lambda e, pyoth=pyoth: e.tensor_copy(
                    pyoth[:], HB[:, 0:256]),
                    ins=[HBk], outs=[pyok, pyok + "y"], label=f"Pup{m}")
            else:
                P.op("scalar", lambda e, pyoth=pyoth: e.copy(
                    pyoth[:], HB[:, 0:256]),
                    ins=[HBk], outs=[pyok, pyok + "y"], label=f"Pup{m}")
            yield
            if stp < 4:
                P.op("tensor", lambda e, xcur=xcur, pycur=pycur, xck=xck, pyck=pyck: e.matmul(
                    HB[:, 256:384], pycur[:, 128:256], xcur[:], start=True, stop=True),
                    ins=[pyck + "y", xck], outs=[HBk], label=f"Xsq{m}")
                yield
                if stp % 2 == 0:
                    P.op("vector", lambda e, xoth=xoth, xok=xok: e.tensor_copy(xoth[:], HB[:, 256:384]),
                         ins=[HBk], outs=[xok], label=f"Xev{m}")
                else:
                    P.op("scalar", lambda e, xoth=xoth, xok=xok: e.activation(
                        xoth[:], HB[:, 256:384], AF.Copy),
                        ins=[HBk], outs=[xok], label=f"Xev{m}")
                yield
            yield
            xcur, xck, xoth, xok = xoth, xok, xcur, xck
            pycur, pyck, pyoth, pyok = pyoth, pyok, pycur, pyck
        P.op("tensor", lambda e, pycur=pycur, pyck=pyck: e.transpose(
            bkb[s01 * 2 + vh][:, 768:896], pycur[:, 128:256], idbf[:]),
            ins=[pyck + "y", "idbf"], outs=[HBk], label="Y64T")
        yield
        P.op("scalar", lambda e: e.activation(Xs[i][:], bkb[s01 * 2 + vh][:, 768:896], AF.Copy),
             ins=[HBk], outs=[f"Xs{i}"], label="Y64Tev")
        yield
        P.op("tensor", lambda e, pycur=pycur, pyck=pyck: e.matmul(
            HB[:, 256:384], Xs[i][:], pycur[:, 0:128], start=True, stop=False),
            ins=[f"Xs{i}", pyck], outs=[HBk], inc=False, label="Pfmm")
        P.op("tensor", lambda e, pycur=pycur, pyck=pyck: e.matmul(
            HB[:, 256:384], idbf[:], pycur[:, 0:128], start=False, stop=True),
            ins=["idbf", pyck], outs=[HBk], label="PfmmI")
        yield
        P.op("vector", lambda e, par=par: e.tensor_copy(Pf[i][par][:], HB[:, 256:384]),
             ins=[HBk], outs=[f"Pf{i}_{par}"], label="Pfin")
        yield

    def tail_chunk(pair, s01, ck, vh, par):
        i = s01 * 2 + vh
        TB = bk[4 + i]
        TBk = f"bk{4 + i}"
        qkey = f"qkvT{s01}"
        zkey = f"zsT{s01}"
        Ktp = lambda: qkvT()[:, (1 * 2 + s01) * SEG + ck * 128: (1 * 2 + s01) * SEG + (ck + 1) * 128]
        Qtp = lambda: qkvT()[:, (0 * 2 + s01) * SEG + ck * 128: (0 * 2 + s01) * SEG + (ck + 1) * 128]
        col = cols[i][:, 16 * par:16 * par + 16]
        ckey = f"cols{i}_{par}"
        P.op("tensor", lambda e: e.matmul(TB[:, 128:256], Ktp(), S_b[i][:], start=True, stop=True),
             ins=[qkey, f"Sb{i}"], outs=[TBk], label="KS0")
        yield
        P.op("vector", lambda e, par=par: e.scalar_tensor_tensor(Rb[i][:], TB[:, 128:256],
                                                                 col[:, MBBS2:MBBS2 + 1],
                                                                 Vbs[i][par][:], ALU.mult, ALU.add),
             ins=[TBk, ckey, f"Vbs{i}_{par}"], outs=[f"Rb{i}"], label="R")
        yield
        yield
        P.op("tensor", lambda e, par=par: e.matmul(TB[:, 128:256], Pf[i][par][:], Rb[i][:],
                                                   start=True, stop=True),
             ins=[f"Pf{i}_{par}", f"Rb{i}"], outs=[TBk], label="umm")
        yield
        P.op("scalar", lambda e: e.activation(Ub[i][:], TB[:, 128:256], AF.Copy),
             ins=[TBk], outs=[f"Ub{i}"], label="uev")
        yield
        P.op("vector", lambda e: e.tensor_scalar(Upb[i][:], Ub[i][:], col[:, BCB:BCB + 1], None, ALU.mult),
             ins=[f"Ub{i}", ckey], outs=[f"Upb{i}"], label="upev")
        yield
        yield
        P.op("tensor", lambda e, par=par: e.matmul(TB[:, 128:256], Gb[i][par][:], Ub[i][:],
                                                   start=True, stop=True),
             ins=[f"Gb{i}_{par}", f"Ub{i}"], outs=[TBk], inc=False, label="Gu")
        P.op("tensor", lambda e: e.matmul(TB[:, 384:512], Qtp(), S_b[i][:], start=True, stop=True),
             ins=[qkey, f"Sb{i}"], outs=[TBk], label="QS0")
        yield
        P.op("vector", lambda e: e.tensor_scalar(Otmp[i][:], TB[:, 384:512], col[:, BB:BB + 1], None, ALU.mult),
             ins=[TBk, ckey], outs=[f"Otmp{i}"], label="Oqs")
        yield
        P.op("vector", lambda e: e.tensor_add(Of[i][:], Otmp[i][:], TB[:, 128:256]),
             ins=[TBk, f"Otmp{i}"], outs=[f"Of{i}"], label="O")
        yield
        yield
        P.op("tensor", lambda e, par=par: e.matmul(TB[:, 128:256], Ktok[i][par][:], Upb[i][:],
                                                   start=True, stop=True),
             ins=[f"Ktok{i}_{par}", f"Upb{i}"], outs=[TBk], label="Smm")
        yield
        P.op("vector", lambda e: e.scalar_tensor_tensor(S_f[i][:], S_f[i][:], col[:, BC:BC + 1],
                                                        TB[:, 128:256], ALU.mult, ALU.add),
             ins=[TBk, ckey, f"Sf{i}"], outs=[f"Sf{i}"], label="Sup")
        yield
        P.op("scalar", lambda e: e.activation(S_b[i][:], S_f[i][:], AF.Copy),
             ins=[f"Sf{i}"], outs=[f"Sb{i}"], label="Sbcp")
        yield
        yield
        P.op("scalar", lambda e: e.activation(Otmp[i][:], Of[i][:], AF.Square,
                                              accum_out=col[:, M2:M2 + 1]),
             ins=[f"Of{i}"], outs=[f"Otmp{i}", ckey], label="Osq")
        yield
        P.op("vector", lambda e: e.tensor_tensor(col[:, OSC:OSC + 1], col[:, SQ:SQ + 1], col[:, SQ:SQ + 1], ALU.mult),
             ins=[ckey], outs=[ckey], label="sq2")
        yield
        P.op("vector", lambda e: e.tensor_scalar(col[:, OSC:OSC + 1], col[:, OSC:OSC + 1], 1.0 / (128.0 * 128.0),
                                                 col[:, M2:M2 + 1], ALU.mult, ALU.mult),
             ins=[ckey], outs=[ckey], label="m2s")
        yield
        P.op("scalar", lambda e: e.activation(col[:, OSC:OSC + 1], col[:, OSC:OSC + 1],
                                              AF.Ln, bias=epsc[:, 1:2]),
             ins=[ckey, "epsc"], outs=[ckey], label="rstd")
        yield
        P.op("scalar", lambda e: e.activation(col[:, OSC:OSC + 1], col[:, OSC:OSC + 1],
                                              AF.Exp, scale=-0.5),
             ins=[ckey], outs=[ckey], label="rstd2")
        yield
        P.op("vector", lambda e: e.tensor_scalar(col[:, OSC:OSC + 1], col[:, OSC:OSC + 1], DKmh,
                                                 col[:, SQ:SQ + 1], ALU.mult, ALU.mult),
             ins=[ckey], outs=[ckey], label="osc")
        yield
        yield
        P.op("vector", lambda e: e.tensor_scalar(Of[i][:], Of[i][:], col[:, OSC:OSC + 1], None, ALU.mult),
             ins=[f"Of{i}", ckey], outs=[f"Of{i}"], label="on1")
        yield
        P.op("gpsimd", lambda e: e.tensor_mul(Of[i][:], Of[i][:], nwb[:]),
             ins=[f"Of{i}", "nwb"], outs=[f"Of{i}"], label="on2")
        yield
        P.op("tensor", lambda e: e.transpose(TB[:, 256:384], Of[i][:], idf[:]),
             ins=[f"Of{i}", "idf"], outs=[TBk], label="OT")
        yield
        osl = (s01 * 2 + vh) * SEG
        P.op("vector", lambda e, osl=osl: e.tensor_mul(ogsb()[:, osl + ck * 128: osl + (ck + 1) * 128],
                                              TB[:, 256:384],
                                              zsT()[:, (vh * 2 + s01) * SEG + ck * 128: (vh * 2 + s01) * SEG + (ck + 1) * 128]),
             ins=[TBk, zkey], outs=[f"og{s01}_{vh}"], label="og")
        yield

    def drive(gens):
        while gens:
            nxt = []
            for g in gens:
                try:
                    next(g)
                    nxt.append(g)
                except StopIteration:
                    pass
            gens = nxt

    def chain_gen(pair, s01):
        tails = []
        for ck in range(NCK):
            par = ck % 2
            h1 = head_chunk(pair, s01, ck, 1, par)
            sub = [head_chunk(pair, s01, ck, 0, par)] + tails
            h1_started = [False]
            while sub:
                keep = []
                for g in sub:
                    try:
                        v = next(g)
                        keep.append(g)
                        if v == "shared_done" and not h1_started[0]:
                            keep.append(h1)
                            h1_started[0] = True
                    except StopIteration:
                        pass
                sub = keep
                yield
            if not h1_started[0]:
                for _ in h1:
                    pass
            tails = [tail_chunk(pair, s01, ck, 0, par), tail_chunk(pair, s01, ck, 1, par)]
        sub = tails
        while sub:
            keep = []
            for g in sub:
                try:
                    next(g)
                    keep.append(g)
                except StopIteration:
                    pass
            sub = keep
            yield


    for pair in range(NSEG):
        for s01 in range(2):
            proj_block(pair, s01)
        bg_acts(pair)
        gens = [chain_gen(pair, 0), chain_gen(pair, 1)]
        while gens:
            nxt = []
            for g in gens:
                try:
                    next(g)
                    nxt.append(g)
                except StopIteration:
                    pass
            gens = nxt
        for s01 in range(2):
            rb = pair + 4 * s01
            for vh in range(2):
                P.dma_pool("sync", lambda e, s01=s01, rb=rb, vh=vh: e.dma_start(
                    out=ogag_in[vh * 128:(vh + 1) * 128, rb * SEG:(rb + 1) * SEG],
                    in_=ogsb()[:, (s01 * 2 + vh) * SEG:(s01 * 2 + vh + 1) * SEG]),
                    ins=[f"og{s01}_{vh}"], outs=[f"ogw{rb}_{vh}"], pool="og", npool=2,
                    label=f"ogw{rb}_{vh}")

    P.op("gpsimd", lambda e: e.collective_compute("AllGather", ALU.bypass, rg, [ogag_in], [ogag_out]),
         ins=[f"ogw{rb}_{vh}" for rb in range(8) for vh in range(2)], outs=["ogag"], label="AGog")

    P.hard_barrier("B2C")

    P.dma("gpsimd", lambda e: e.dma_start(
        out=woutb().rearrange("p (vc c) -> p vc c", c=D),
        in_=wout_in.rearrange("(vc p) c -> p vc c", p=128)), outs=["woutb"], role="g6", label="wld")
    _selvals = {}

    def selval(e):
        if "v" not in _selvals:
            _selvals["v"] = e.partition_id() * NT
        return _selvals["v"]

    ogt2 = [a1b(ogt2_o, 16 * 128), a1b(ogt2_o + 16 * 128 // 2, 16 * 128)]

    def c1_gen(t):
        par = t % 2
        xb, xkey = (x0t, "x0A") if par == 0 else (x0t2, "x0B")
        og, ogkey = ogt2[par], f"ogtT{par}"
        base = 4 * par
        P.dma_pool("sync", lambda e, t=t, og=og: e.dma_start(
            out=og.rearrange("p (vc c) -> p vc c", c=128),
            in_=ogag_out.rearrange("(vc p) tt -> p vc tt", p=128)[:, :, t * 128:][:, :, bass.ts(selval(e), 128)]),
            ins=["ogag"], outs=[ogkey], pool="oc", npool=2, label=f"ogr{t}")
        P.dma("sync", lambda e, t=t, xb=xb: e.dma_start(out=xb[:], in_=x1sp[t * 128:(t + 1) * 128, :]),
              ins=[f"x1sp{t}"], outs=[xkey], role=f"x{par}", label=f"x1ld{t}")
        yield
        for eb in range(ne):
            w = min(512, D - eb * 512)
            for vc in range(16):
                P.op("tensor", lambda e, vc=vc, w=w, eb=eb, og=og: e.matmul(
                    bk[base + eb][:, 0:w], og[:, vc * 128:(vc + 1) * 128],
                    woutb()[:, vc * D + eb * 512: vc * D + eb * 512 + w],
                    start=(vc == 0), stop=(vc == 15)),
                    ins=[ogkey, "woutb"], outs=[f"bk{base + eb}"], inc=(vc == 15), label="womm")
            yield
        for eb in range(ne):
            w = min(512, D - eb * 512)
            P.op("vector", lambda e, eb=eb, w=w, xb=xb: e.tensor_add(
                xb[:, eb * 512:eb * 512 + w], bk[base + eb][:, 0:w], xb[:, eb * 512:eb * 512 + w]),
                ins=[f"bk{base + eb}", xkey], outs=[xkey], label="x2add")
            yield
        P.dma("sync", lambda e, t=t, xb=xb: e.dma_start(out=x2sp[t * 128:(t + 1) * 128, :], in_=xb[:]),
              ins=[xkey], outs=[f"x2sp{t}"], role="x2w", label=f"x2w{t}")
        yield

    for t0_ in range(0, NT, 2):
        dr = [c1_gen(t0_), c1_gen(t0_ + 1)] if t0_ + 1 < NT else [c1_gen(t0_)]
        while dr:
            keep = []
            for g in dr:
                try:
                    next(g)
                    keep.append(g)
                except StopIteration:
                    pass
            dr = keep

    P.hard_barrier("C12")
    for r in range(NCORES):
        P.dma("gpsimd", lambda e, r=r: e.dma_start(
            out=fsumC().rearrange("p (dc rr c) -> p dc rr c", rr=NCORES, c=FS)[:, :, r, :],
            in_=fsag_out[r * D:(r + 1) * D, :].rearrange("(dc p) c -> p dc c", p=128)),
            ins=["fsag_out"], outs=["fsumb"], role=f"g{4 + r % 2}", label=f"fsld2{r}")

    def c2_gen(t):
        par = t % 2
        xb, xkey = (x0t, "x0A") if par == 0 else (x0t2, "x0B")
        hb, hkey2 = hbf_p[par], f"hbfp{par}"
        ht, htkey = htT_p[par], f"htTp{par}"
        xo, xokey = x1t_p[par], f"x1tp{par}"
        base = 4 * par
        P.dma("sync", lambda e, t=t, xb=xb: e.dma_start(out=xb[:], in_=x2sp[t * 128:(t + 1) * 128, :]),
              ins=[f"x2sp{t}"], outs=[xkey], role=f"x{par}", label=f"x2ld{t}")
        yield
        yield from ln_tile_gen(xkey, xb, hb, hkey2, f"C{t}", par)
        yield from transpose_gen(hb, hkey2, ht, htkey, DC, base + 3, par)
        for eb in range(ne):
            w = min(512, D - eb * 512)
            for dcb in range(DC):
                P.op("tensor", lambda e, eb=eb, w=w, dcb=dcb, ht=ht: e.matmul(
                    bk[base + eb][:, 0:w], ht[:, dcb * 128:(dcb + 1) * 128],
                    fsumC()[:, dcb * D + eb * 512: dcb * D + eb * 512 + w],
                    start=(dcb == 0), stop=(dcb == DC - 1)),
                    ins=[htkey, "fsumb"], outs=[f"bk{base + eb}"], inc=(dcb == DC - 1), label="ffn2mm")
            yield
        P.op("gpsimd", lambda e, xb=xb: e.tensor_add(xb[:], xb[:], biasb[:]),
             ins=[xkey, "biasb"], outs=[xkey], label="ffn2bias")
        yield
        for eb in range(ne):
            w = min(512, D - eb * 512)
            P.op("vector", lambda e, eb=eb, w=w, xb=xb, xo=xo: e.tensor_add(
                xo[:, eb * 512:eb * 512 + w], bk[base + eb][:, 0:w], xb[:, eb * 512:eb * 512 + w]),
                ins=[f"bk{base + eb}", xkey], outs=[xokey], label="ffn2res")
            yield
        P.dma("sync", lambda e, t=t, xo=xo: e.dma_start(out=ysh[t * 128:(t + 1) * 128, :], in_=xo[:]),
              ins=[xokey], outs=[f"y{t}"], role="yo", label=f"yout{t}")
        yield

    for t0_ in range(0, NT, 2):
        dr = [c2_gen(t0_), c2_gen(t0_ + 1)] if t0_ + 1 < NT else [c2_gen(t0_)]
        while dr:
            keep = []
            for g in dr:
                try:
                    next(g)
                    keep.append(g)
                except StopIteration:
                    pass
            dr = keep

    if dbg:
        P.hard_barrier("dbgout")
        P.dma("sync", lambda e: e.dma_start(
            out=dbgt.rearrange("s p c -> p s c"), in_=dbgsb[:].rearrange("p (s c) -> p s c", c=128)),
            ins=["dbgsb"], outs=["dbgdram"], role="dbg", label="dbgout")
    P.final_waits("sync")

    sems = {}
    for name in P.used_sems():
        sems[name] = es.enter_context(nc.semaphore(name))
    P.maxop = maxop
    block = es.enter_context(nc.Block())
    P.replay(nc, block, sems)
    es.close()
    meta = dict(o_h2=o_h2, o_qkvT=o_qkvT, o_zsT=o_zsT, o_stag=o_stag, o_ogsb=o_ogsb,
                o_convd=o_convd, SEG=SEG, NT=NT, NCK=NCK, TCH=TCH, DC=DC, a1w=a1w)
    return nc, len(P.ops), meta


def make_inputs(core, inp, T, D):
    b = core // 4
    SEG = T // 4
    t0 = SEG * (core % 4)
    KEY = 1024
    x = np.ascontiguousarray(np.asarray(inp["L0_attn_hidden_states"])[b, t0:t0 + SEG, :], dtype=np.float32)
    wqkv = np.asarray(inp["L0_attn_W_qkv"])
    qcol = slice(128 * core, 128 * core + 128)
    kcol = slice(KEY + 128 * core, KEY + 128 * core + 128)
    vcol = slice(2 * KEY + 256 * core, 2 * KEY + 256 * core + 256)
    wqkv_s = np.ascontiguousarray(np.concatenate(
        [wqkv[:, qcol], wqkv[:, kcol], wqkv[:, vcol]], axis=1), dtype=np.float32)
    wz_s = np.ascontiguousarray(np.asarray(inp["L0_attn_W_z"])[:, 256 * core:256 * core + 256], dtype=np.float32)
    wb = np.asarray(inp["L0_attn_W_b"])[:, 2 * core:2 * core + 2]
    wa = np.asarray(inp["L0_attn_W_a"])[:, 2 * core:2 * core + 2]
    wab_s = np.ascontiguousarray(np.concatenate([wb, wa], axis=1), dtype=np.float32)
    FS = D // 8
    fact_s = np.ascontiguousarray(np.asarray(inp["L0_ffn_factors"])[:, :, FS * core:FS * core + FS],
                                  dtype=np.float32)
    convw = np.asarray(inp["L0_attn_conv_w"])
    conv_rows = np.concatenate([convw[qcol.start:qcol.stop], convw[kcol.start:kcol.stop],
                                convw[vcol.start:vcol.stop]], axis=0)
    dtb = np.asarray(inp["L0_attn_dt_bias"])[2 * core:2 * core + 2].reshape(1, 2)
    alog = np.asarray(inp["L0_attn_A_log"])[2 * core:2 * core + 2].reshape(1, 2)

    ii = np.arange(128)
    jj = np.arange(128)
    Umat = (ii[:, None] <= jj[None, :]).astype(np.float32)
    masku = (BIG * (jj[None, :] >= ii[:, None])).astype(np.float32)
    maskl = (-BIG * (jj[None, :] < ii[:, None])).astype(np.float32)

    return {
        "xsh": x,
        "wqkv_s": wqkv_s,
        "wz_s": wz_s,
        "wab_s": wab_s,
        "fact_s": fact_s,
        "fbias": np.ascontiguousarray(np.asarray(inp["L0_ffn_bias"]).reshape(1, D), dtype=np.float32),
        "wout_in": np.ascontiguousarray(np.asarray(inp["L0_attn_W_out"]), dtype=np.float32),
        "convw_s": np.ascontiguousarray(conv_rows, dtype=np.float32),
        "dtb_s": np.ascontiguousarray(dtb, dtype=np.float32),
        "alog_s": np.ascontiguousarray(alog, dtype=np.float32),
        "normw": np.ascontiguousarray(np.asarray(inp["L0_attn_norm_weight"]).reshape(1, 128), dtype=np.float32),
        "IDbf": np.eye(128, dtype=ml_dtypes.bfloat16),
        "IDf": np.eye(128, dtype=np.float32),
        "UINC": Umat,
        "MASKU": masku,
        "MASKL": maskl,
        "ONESROW": np.ones((1, 128), dtype=np.float32),
        "ONECOL": np.ones((128, 1), dtype=np.float32),
    }


def kernel(**inputs):
    T, D = 8192, 2048
    nc, _nops, _meta = build_nc(T=T, D=D)
    in_maps = [make_inputs(c, inputs, T, D) for c in range(NCORES)]
    res = run_bass_kernel_spmd(nc, in_maps, core_ids=list(range(NCORES)))
    SEG = T // 4
    out = np.empty((2, T, D), dtype=np.float32)
    for c in range(NCORES):
        b = c // 4
        t0 = SEG * (c % 4)
        out[b, t0:t0 + SEG, :] = res.results[c]["ysh"]
    return out
